# version 10
# speedup vs baseline: 1.3653x; 1.0307x over previous
import numpy as np
_LAST_SCHED = None
import ml_dtypes
import concourse.bass as bass
import concourse.mybir as mybir
from concourse.bass_utils import run_bass_kernel_spmd
from concourse.ap import AP

F32 = mybir.dt.float32
BF16 = mybir.dt.bfloat16
AF = mybir.ActivationFunctionType
ALU = mybir.AluOpType
AX = mybir.AxisListType

B, T, D = 4, 8192, 1024
OWN = 4096
WIN = 512
HID, HIDP = 2730, 2816
W1C = 2 * HIDP
NKC2 = 22
XN_N = 8192
NHIST = 4
FA0_Q0, FA0_NQ = -896, 39
FB0_Q0, FB0_NQ = -768, 38
FA1_Q0, FA1_NQ = -128, 33
FB1_Q0, FB1_NQ = 0, 32
X1_N = FA0_NQ * 128
X2_N = FB0_NQ * 128
X3_N = FA1_NQ * 128
TAB0 = -1408
TAB_N = 5504
NMASK = FA0_NQ + FA1_NQ

ENGINES = ("sync", "vector", "scalar", "tensor", "gpsimd")


class Sched:
    def __init__(self, nc):
        self.nc = nc
        self.q = {e: [] for e in ENGINES}
        self.marks = []
        self.sems = {}
        self.c = {}
        self.dma_chans = set()
        for n in ("pe", "ve", "ac", "gp", "bar"):
            self.sems[n] = nc.alloc_semaphore(n)
            self.c[n] = 0

    def chan(self, n):
        if n not in self.sems:
            self.sems[n] = self.nc.alloc_semaphore(n)
            self.c[n] = 0
            self.dma_chans.add(n)
        return n

    def emit(self, eng, fn, wait=(), inc=None, inc_k=1):
        w = tuple((s, v) for t_ in wait if t_ is not None for (s, v) in (t_,) if v > 0)
        tok = None
        if inc is not None:
            self.c[inc] += inc_k
            tok = (inc, self.c[inc])
        self.q[eng].append((w, fn, (self.sems[inc], inc_k) if inc else None,
                            eng in ("vector", "scalar", "gpsimd")))
        return tok

    def dma(self, eng, out, in_, wait=(), ch="wc"):
        sem = self.chan(ch)
        return self.emit(eng, lambda e, o=out, i=in_: e.dma_start(out=o, in_=i),
                         wait=wait, inc=sem, inc_k=16)

    def barrier(self):
        self.marks.append({e: len(self.q[e]) for e in ENGINES})
        self.c["bar"] += 5
        tgt = self.c["bar"]
        for eng in ENGINES:
            if eng == "sync":
                w = tuple((n, self.c[n]) for n in sorted(self.dma_chans)
                          if self.c[n] > 0)
            else:
                w = ()
            self.q[eng].append((w, lambda e: e.nop(), (self.sems["bar"], 1), False))
            self.q[eng].append(((("bar", tgt),), None, None, False))

    def realize(self):
        sems = self.sems
        with self.nc.Block() as block:
            def mk(engname):
                def body(e):
                    for w, fn, inc, drn in self.q[engname]:
                        for s, v in w:
                            e.wait_ge(sems[s], v)
                        if fn is None:
                            continue
                        ins = fn(e)
                        if inc is not None:
                            if ins is None:
                                if drn:
                                    e.drain()
                                ins = e.nop()
                            ins.then_inc(*inc)
                        if drn:
                            e.drain()
                return body
            block.sync(mk("sync"))
            block.vector(mk("vector"))
            block.scalar(mk("scalar"))
            block.tensor(mk("tensor"))
            block.gpsimd(mk("gpsimd"))


class Arena:
    def __init__(self, nc, base):
        self.nc = nc
        self.base = base
        self.cur = base
        self.n = 0

    def reset(self):
        self.cur = self.base

    def at(self, shape, dtype):
        nb = int(np.prod(shape[1:])) * mybir.dt.size(dtype)
        nb = (nb + 31) & ~31
        off = self.cur
        self.cur += nb
        self.n += 1
        assert self.cur <= getattr(self, "limit", 207 * 1024), \
            f"SBUF arena overflow: {self.cur - self.base}"
        return self.nc.alloc_sbuf_tensor_at(f"ar{self.n}", list(shape), dtype,
                                            offset=off)


def apm(base_ap, extra_off, dims):
    return AP(base_ap.tensor, base_ap.offset + extra_off, dims)


def build_nc(dump=(), upto=6):
    nc = bass.Bass("TRN2", target_bir_lowering=False, debug=False)

    def di(name, shape, dtype=F32):
        return nc.dram_tensor(name, list(shape), dtype, kind="ExternalInput")

    def dsc(name, shape, dtype):
        kind = "ExternalOutput" if name in dump else "Internal"
        return nc.dram_tensor(name, list(shape), dtype, kind=kind)

    x_all = di("x_all", [XN_N, D])
    w_attn = di("w_attn", [3072, D])
    w_proj = di("w_proj", [D, D])
    wqkv = di("wqkv", [2, D, 1536])
    wo = di("wo", [2, 512, D])
    w1p = di("w1p", [2, D, W1C])
    w2p = di("w2p", [2, HIDP, D])
    b1p = di("b1p", [2, 128, 44])
    lbv = di("lbv", [2, 128, 8], BF16)
    lgv = di("lgv", [2, 128, 8])
    b2v = di("b2v", [2, 128, 8])
    qsr = di("qsr", [2, 1, 64])
    ksr = di("ksr", [2, 1, 64])
    tabs = {n: di(n, [TAB_N, 64]) for n in ("csq", "ssq", "csk", "ssk")}
    mks = di("mks", [NMASK, 128, 640], BF16)
    ident_b = di("ident_b", [128, 128], BF16)
    ident_f = di("ident_f", [128, 128])
    trilT_i = di("trilT", [128, 128], BF16)
    smask = di("smask", [1, 16], BF16)
    one16 = di("one16", [1, 16], BF16)
    crow = di("crow", [1, 8])
    out_t = nc.dram_tensor("out", [OWN, D], F32, kind="ExternalOutput")

    xn_sc = dsc("xn_sc", [D, XN_N], BF16)
    ao_sc = dsc("ao_sc", [D, OWN], BF16)
    x1f_sc = dsc("x1f_sc", [D, X1_N], F32)
    x1b_sc = dsc("x1b_sc", [D, X1_N], BF16)
    x2f_sc = dsc("x2f_sc", [D, X2_N], F32)
    x2b_sc = dsc("x2b_sc", [D, X2_N], BF16)
    x3f_sc = dsc("x3f_sc", [D, X3_N], F32)
    x3b_sc = dsc("x3b_sc", [D, X3_N], BF16)
    rt_sc = dsc("rt_sc", [1, 2048], F32)
    dn_sc = dsc("dn_sc", [32, 128], BF16)
    ac_sc = dsc("ac_sc", [1, 1024], F32)

    def fm_read(t, ncols, c0, w, nch=8, ch0=0):
        return AP(t, ch0 * 128 * ncols + c0,
                  [[ncols, 128], [128 * ncols, nch], [1, w]])

    def fm_write(t, ncols, c0, w, nch=8, ch0=0):
        return AP(t, ch0 * 128 * ncols + c0,
                  [[ncols, 128], [128 * ncols, nch], [1, w]])

    ps_a = nc.alloc_psum_tensor("ps_a", [128, 512], F32)
    ps_b = nc.alloc_psum_tensor("ps_b", [128, 512], F32)
    ps_c = nc.alloc_psum_tensor("ps_c", [128, 512], F32)
    ps_d = nc.alloc_psum_tensor("ps_d", [128, 512], F32)
    ps_E2 = nc.alloc_psum_tensor("ps_E2", [128, 1024], F32)
    ps_G2 = nc.alloc_psum_tensor("ps_G2", [128, 1024], F32)

    P = nc.alloc_sbuf_tensor
    idb = P("idb", [128, 128], BF16)
    idf = P("idf", [128, 128], F32)
    trilT = P("trilT_s", [128, 128], BF16)
    cst = P("cst", [128, 8], F32)
    ones1 = P("ones1", [128, 1], BF16)
    ARENA_BYTES = 208896
    spacer = P("arena_spacer", [128, ARENA_BYTES], mybir.dt.uint8)
    arena_base = nc.lookup_mloc(spacer).addr

    S = Sched(nc)
    global _LAST_SCHED
    _LAST_SCHED = S
    AR = Arena(nc, arena_base)
    AR.limit = arena_base + ARENA_BYTES

    S.dma("sync", idb[:], ident_b.ap())
    S.dma("sync", idf[:], ident_f.ap())
    S.dma("sync", trilT[:], trilT_i.ap())
    S.dma("sync", cst[:], AP(crow, 0, [[0, 128], [1, 8]]))
    t_pre = S.dma("sync", ones1[:], AP(one16, 0, [[0, 128], [1, 1]]))

    assert upto >= 1
    xst = [AR.at([128, 1024], F32) for _ in range(4)]
    bn6 = AR.at([128, 12], F32)
    bn2 = AR.at([128, 2], F32)
    rstd = AR.at([128, 1], F32)
    nmr = AR.at([128, 1], F32)
    xnt = [AR.at([128, 1024], BF16) for _ in range(4)]
    xns = [AR.at([128, 1024], BF16) for _ in range(4)]
    done_x = [None] * 4
    done_tr = [None, None, None, None]
    done_st = [None, None, None, None]
    prev_t2 = None
    for j in range(64):
        sl = j % 4
        ld = S.dma("sync", xst[sl][:], AP(x_all, j * 128 * D, [[D, 128], [1, D]]),
                   wait=[done_x[sl], t_pre if j == 0 else None], ch=f"x{sl}")
        def f_bn(e, s=sl):
            e.bn_stats(out=bn6[:, 0:6], in_=xst[s][:, 0:512])
            e.drain()
            e.bn_stats(out=bn6[:, 6:12], in_=xst[s][:, 512:1024])
            e.drain()
            return e.bn_aggr(out=bn2[:], in_=bn6[:])
        t1 = S.emit("vector", f_bn, wait=[ld, prev_t2], inc="ve")
        t2a = S.emit("scalar", lambda e: e.activation(out=rstd[:], in_=bn2[:, 1:2],
                     func=AF.Sqrt, bias=cst[:, 0:1]), wait=[t1], inc="ac")
        t2 = S.emit("vector", lambda e: e.reciprocal(out=rstd[:], in_=rstd[:]),
                    wait=[t2a], inc="ve")
        prev_t2 = t2
        t3 = S.emit("vector", lambda e: e.tensor_scalar(out=nmr[:], in0=bn2[:, 0:1],
                    scalar1=rstd[:], scalar2=-1.0, op0=ALU.mult, op1=ALU.mult),
                    wait=[t2], inc="ve")
        pj = j % 4
        t4 = S.emit("vector", lambda e, s=sl, p=pj: e.tensor_scalar(
            out=xnt[p][:], in0=xst[s][:], scalar1=rstd[:], scalar2=nmr[:],
            op0=ALU.mult, op1=ALU.add), wait=[t3, done_tr[pj]], inc="ve")
        done_x[sl] = t4
        pv = (ps_G2[:].bitcast(BF16)[:, 0:1024], ps_G2[:].bitcast(BF16)[:, 1024:2048],
              ps_E2[:].bitcast(BF16)[:, 0:1024], ps_E2[:].bitcast(BF16)[:, 1024:2048])[pj]
        def f_tr(e, p=pj, pv=pv):
            for fc in range(8):
                ins_ = e.transpose(out=pv[:, fc * 128: (fc + 1) * 128],
                            in_=xnt[p][:, fc * 128: (fc + 1) * 128], identity=idb[:])
            return ins_
        t5 = S.emit("tensor", f_tr, wait=[t4, done_st[pj]], inc="pe")
        t6 = S.emit("scalar", lambda e, p=pj, pv=pv: e.copy(out=xns[p][:],
                    in_=pv[:, 0:1024]), wait=[t5], inc="ac")
        done_tr[pj] = t6
        done_st[pj] = S.dma("sync", fm_write(xn_sc, XN_N, j * 128, 128),
                            xns[pj][:].rearrange("p (c t) -> p c t", c=8), wait=[t6],
                            ch=f"st{pj}")
    S.barrier()

    if upto < 2:
        S.q["sync"].append((tuple((n, S.c[n]) for n in sorted(S.dma_chans)
                                  if S.c[n] > 0), None, None, False))
        S.realize()
        return nc
    AR.reset()
    wT = [AR.at([128, 3072], BF16) for _ in range(8)]
    wTp = [AR.at([128, 1024], BF16) for _ in range(8)]
    wst = [AR.at([128, 1024], F32) for _ in range(2)]
    wsb = AR.at([128, 1024], BF16)
    last_tr = t_pre
    prev_cast = None
    for widx, (wsrc, nrc, dst) in enumerate(((w_attn, 24, wT), (w_proj, 8, wTp))):
        for rc in range(nrc):
            sl = rc % 2
            ld = S.dma("sync", wst[sl][:], AP(wsrc, rc * 128 * D, [[D, 128], [1, D]]),
                       wait=[prev_cast])
            tc_ = S.emit("vector", lambda e, s=sl: e.tensor_copy(out=wsb[:], in_=wst[s][:]),
                         wait=[ld, last_tr], inc="ve")
            prev_cast = tc_
            pview = ps_G2[:].bitcast(BF16)[:, 0:1024]
            def f_tw(e):
                for ec in range(8):
                    ins_ = e.transpose(out=pview[:, ec * 128: (ec + 1) * 128],
                                in_=wsb[:, ec * 128: (ec + 1) * 128], identity=idb[:])
                return ins_
            t5 = S.emit("tensor", f_tw, wait=[tc_], inc="pe")
            def f_ev(e, rc=rc, dst=dst):
                for ec in range(8):
                    ins_ = e.copy(out=dst[ec][:, rc * 128: (rc + 1) * 128],
                           in_=pview[:, ec * 128: (ec + 1) * 128])
                return ins_
            last_tr = S.emit("scalar", f_ev, wait=[t5], inc="ac")

    xnt_a = [AR.at([128, 1024], BF16) for _ in range(2)]
    phik_tm1 = AR.at([128, 1024], BF16)
    v_ext1 = AR.at([128, 1040], BF16)
    tmpf = AR.at([128, 512], F32)
    tmpe = AR.at([128, 512], F32)
    SzA = ps_E2[0:128, 0:260]
    SzB = ps_E2[0:128, 512:772]

    def vext_ones(th, ntile=1):
        pitch = th[:].ap[0][0]
        if ntile == 1:
            return AP(th, 64, [[pitch, 128], [65, 16], [1, 1]])
        return AP(th, 64, [[pitch, 128], [1040, ntile], [65, 16], [1, 1]])

    smask_sb = AR.at([128, 16], BF16)
    t_vm0 = S.dma("sync", smask_sb[:], AP(smask, 0, [[0, 128], [1, 16]]),
                  wait=[last_tr])
    t_vm = S.emit("vector", lambda e: e.tensor_copy(
        out=vext_ones(v_ext1),
        in_=AP(smask_sb, 0, [[16, 128], [1, 16], [0, 1]])), wait=[t_vm0], inc="ve")

    def phi_chain(psrc, dstap, wait_t):
        a1 = S.emit("vector", lambda e: e.tensor_scalar_min(out=tmpf[:], in0=psrc,
                    scalar1=0.0), wait=wait_t, inc="ve")
        a2 = S.emit("scalar", lambda e: e.activation(out=tmpe[:], in_=tmpf[:],
                    func=AF.Exp), wait=[a1], inc="ac")
        a3 = S.emit("vector", lambda e, d=dstap: e.scalar_tensor_tensor(out=d,
                    in0=psrc, scalar=0.0, in1=tmpe[:], op0=ALU.max, op1=ALU.add),
                    wait=[a2], inc="ve")
        return a3

    ev_done = t_vm
    for j in range(32):
        ld = S.dma("sync", xnt_a[j % 2][:].rearrange("p (c t) -> p c t", c=8),
                   fm_read(xn_sc, XN_N, j * 128, 128), wait=[ev_done], ch="a1x")
        def f_kv(e, sl=j % 2):
            for fc in range(8):
                lhs = xnt_a[sl][:, fc * 128: (fc + 1) * 128]
                ins_ = e.matmul(out=ps_a[:], lhsT=lhs, rhs=wT[fc][:, 1024:1536], start=fc == 0, stop=fc == 7)
                ins_ = e.matmul(out=ps_b[:], lhsT=lhs, rhs=wT[fc][:, 1536:2048], start=fc == 0, stop=fc == 7)
                ins_ = e.matmul(out=ps_c[:], lhsT=lhs, rhs=wT[fc][:, 2048:2560], start=fc == 0, stop=fc == 7)
                ins_ = e.matmul(out=ps_d[:], lhsT=lhs, rhs=wT[fc][:, 2560:3072], start=fc == 0, stop=fc == 7)
            return ins_
        mm1 = S.emit("tensor", f_kv, wait=[ld, ev_done], inc="pe")
        p1 = phi_chain(ps_a[:], phik_tm1[:, 0:512], [mm1])
        p2 = phi_chain(ps_b[:], phik_tm1[:, 512:1024], [p1])
        pt1 = phik_tm1[:].ap[0][0]
        e1 = S.emit("scalar", lambda e: e.copy(
            out=AP(v_ext1, 0, [[1040, 128], [65, 8], [1, 64]]), in_=ps_c[:]),
            wait=[mm1], inc="ac")
        e2 = S.emit("scalar", lambda e: e.copy(
            out=AP(v_ext1, 8 * 65, [[1040, 128], [65, 8], [1, 64]]), in_=ps_d[:]),
            wait=[e1], inc="ac")
        def f_sz(e, j=j):
            for h in range(16):
                pj = h // 2
                dst = SzA if pj < 4 else SzB
                reg = dst[64 * (h % 2): 64 * (h % 2) + 64,
                          65 * (pj % 4): 65 * (pj % 4) + 65]
                ins_ = e.matmul(out=reg, lhsT=phik_tm1[:, 64 * h: 64 * h + 64],
                         rhs=v_ext1[:, 65 * h: 65 * h + 65],
                         start=(j == 0 and h % 8 < 2), stop=False,
                         skip_group_check=True)
            return ins_
        ev_done = S.emit("tensor", f_sz, wait=[p2, e2], inc="pe")

    xn_U = AR.at([128, 4096], BF16)
    phiq_fm = AR.at([128, 4096], BF16)
    phik_fm = AR.at([128, 4096], BF16)
    phik_tm4 = AR.at([128, 4096], BF16)
    v_ext4 = AR.at([128, 4160], BF16)
    Szb = AR.at([128, 520], BF16)
    sTp = [AR.at([128, 512], BF16) for _ in range(4)]
    y_fm = AR.at([128, 4096], BF16)
    rden_sb = AR.at([128, 2048], F32)
    rden_bc = AR.at([128, 2048], F32)
    ao_st = AR.at([128, 4096], BF16)

    t_vo2 = S.emit("vector", lambda e: e.memset(vext_ones(v_ext4, 4), 1.0),
                   wait=[ev_done], inc="ve")

    prev_scan_pe = ev_done
    prev_chunk_y = None
    prev_proj = t_vo2
    prev_d1 = None
    for u in range(8):
        ldU = S.dma("sync", xn_U[:].rearrange("p (c t) -> p c t", c=8),
                    fm_read(xn_sc, XN_N, 4096 + u * 512, 512), wait=[prev_proj],
                    ch="ldu")
        qk_done = []
        for oc in range(16):
            dstp = ps_G2[:, (oc % 2) * 512:(oc % 2) * 512 + 512]
            wprev = qk_done[-2] if len(qk_done) >= 2 else prev_proj
            def f_qk(e, oc=oc, dstp=dstp):
                for fc in range(8):
                    ins_ = e.matmul(out=dstp, lhsT=wT[fc][:, oc * 128: (oc + 1) * 128],
                             rhs=xn_U[:, fc * 512: (fc + 1) * 512],
                             start=fc == 0, stop=fc == 7)
                return ins_
            m = S.emit("tensor", f_qk, wait=[ldU, wprev], inc="pe")
            dst = (phiq_fm if oc < 8 else phik_fm)[:, (oc % 8) * 512:((oc % 8) + 1) * 512]
            qk_done.append(phi_chain(dstp, dst, [m]))
        va_done = []
        for tc in range(4):
            def f_v(e, tc=tc):
                for fc in range(8):
                    lhs = xn_U[:, fc * 512 + tc * 128: fc * 512 + tc * 128 + 128]
                    ins_ = e.matmul(out=ps_a[:], lhsT=lhs, rhs=wT[fc][:, 2048:2560],
                             start=fc == 0, stop=fc == 7)
                    ins_ = e.matmul(out=ps_b[:], lhsT=lhs, rhs=wT[fc][:, 2560:3072],
                             start=fc == 0, stop=fc == 7)
                return ins_
            m = S.emit("tensor", f_v, wait=[qk_done[-1] if tc == 0 else va_done[-1]],
                       inc="pe")
            c1 = S.emit("scalar", lambda e, tc=tc: e.copy(
                out=AP(v_ext4, tc * 1040, [[4160, 128], [65, 8], [1, 64]]),
                in_=ps_a[:]), wait=[m], inc="ac")
            c2 = S.emit("scalar", lambda e, tc=tc: e.copy(
                out=AP(v_ext4, tc * 1040 + 8 * 65, [[4160, 128], [65, 8], [1, 64]]),
                in_=ps_b[:]), wait=[c1], inc="ac")
            pview = ps_G2[:].bitcast(BF16)[:, 0:1024]
            def f_tk(e, tc=tc):
                for fc in range(8):
                    ins_ = e.transpose(out=pview[:, fc * 128: (fc + 1) * 128],
                                in_=phik_fm[:, fc * 512 + tc * 128: fc * 512 + tc * 128 + 128],
                                identity=idb[:])
                return ins_
            mt = S.emit("tensor", f_tk, wait=[c2, qk_done[-1]], inc="pe")
            ev = S.emit("scalar", lambda e, tc=tc: e.copy(
                out=phik_tm4[:, tc * 1024: (tc + 1) * 1024],
                in_=pview[:, 0:1024]), wait=[mt], inc="ac")
            va_done.append(ev)
        for cc in range(4):
            c = u * 4 + cc
            se1 = S.emit("vector", lambda e: e.tensor_copy(out=Szb[:, 0:260], in_=SzA),
                         wait=[prev_scan_pe, va_done[-1]], inc="ve")
            se2 = S.emit("vector", lambda e: e.tensor_copy(out=Szb[:, 260:520], in_=SzB),
                         wait=[], inc="ve")
            mask_toks = []
            nd = [ps_c[0:65, :], ps_d[0:65, :], ps_G2[0:65, 0:512], ps_G2[0:65, 512:1024]]
            pe_last = None
            nd_funcs = []
            for h in range(16):
                pj, par = h // 2, 64 * (h % 2)
                ping = (ps_a if h % 2 == 0 else ps_b)[0:128, 0:128]
                wait_st = [se2] if h == 0 else []
                if h >= 2:
                    wait_st.append(mask_toks[h - 2])
                if h < 2 and prev_chunk_y is not None:
                    wait_st.append(prev_chunk_y)
                def f_st(e, ping=ping, cc=cc, pj=pj, par=par):
                    return e.matmul(out=ping,
                             lhsT=phik_fm[par:par + 64, pj * 512 + cc * 128: pj * 512 + cc * 128 + 128],
                             rhs=phiq_fm[par:par + 64, pj * 512 + cc * 128: pj * 512 + cc * 128 + 128],
                             start=True, stop=True)
                tst = S.emit("tensor", f_st, wait=wait_st, inc="pe")
                tm = S.emit("vector", lambda e, h=h, ping=ping: e.tensor_tensor(
                    out=sTp[h // 4][:, 128 * (h % 4): 128 * (h % 4) + 128],
                    in0=ping, in1=trilT[:], op=ALU.mult), wait=[tst], inc="ve")
                mask_toks.append(tm)
                reg = nd[h // 4][0:65, 128 * (h % 4): 128 * (h % 4) + 128]
                def f_nd(e, h=h, cc=cc, pj=pj, par=par, reg=reg, c=c):
                    skp = (h % 2 == 1)
                    ins_ = e.matmul(out=reg, lhsT=v_ext4[:, cc * 1040 + 65 * h: cc * 1040 + 65 * h + 65],
                             rhs=sTp[h // 4][:, 128 * (h % 4): 128 * (h % 4) + 128],
                             start=(h % 4 < 2), stop=False, skip_group_check=skp)
                    ins_ = e.matmul(out=reg, lhsT=Szb[par:par + 64, 65 * pj: 65 * pj + 65],
                             rhs=phiq_fm[par:par + 64, pj * 512 + cc * 128: pj * 512 + cc * 128 + 128],
                             start=False, stop=(h % 4 >= 2), skip_group_check=skp)
                    dstz = SzA if pj < 4 else SzB
                    regz = dstz[par:par + 64, 65 * (pj % 4): 65 * (pj % 4) + 65]
                    ins_ = e.matmul(out=regz, lhsT=phik_tm4[:, cc * 1024 + 64 * h: cc * 1024 + 64 * h + 64],
                             rhs=v_ext4[:, cc * 1040 + 65 * h: cc * 1040 + 65 * h + 65],
                             start=False, stop=False, skip_group_check=True)
                    return ins_
                nd_funcs.append((f_nd, tm))
                if h >= 1:
                    fn_, tm_ = nd_funcs[h - 1]
                    pe_last = S.emit("tensor", fn_, wait=[tm_], inc="pe")
            fn_, tm_ = nd_funcs[15]
            pe_last = S.emit("tensor", fn_, wait=[tm_], inc="pe")
            prev_scan_pe = pe_last
            rc1 = None
            for bk in range(4):
                src = nd[bk][64:65, 0:512]
                rc0 = S.emit("scalar", lambda e, bk=bk, src=src: e.activation(
                    out=rden_sb[64:65, 512 * bk: 512 * bk + 512], in_=src,
                    func=AF.Ln, bias=cst[64:65, 1:2]),
                    wait=[pe_last if bk == 0 else None, prev_d1 if bk == 0 else None],
                    inc="ac")
                rc1 = S.emit("scalar", lambda e, bk=bk: e.activation(
                    out=rden_sb[64:65, 512 * bk: 512 * bk + 512],
                    in_=rden_sb[64:65, 512 * bk: 512 * bk + 512],
                    func=AF.Exp, scale=cst[64:65, 4:5]), wait=[rc0], inc="ac")
            d1 = S.dma("sync", rt_sc.ap(), rden_sb[64:65, :], wait=[rc1], ch="rt1")
            prev_d1 = d1
            d2 = S.dma("sync", rden_bc[:], AP(rt_sc, 0, [[0, 128], [1, 2048]]),
                       wait=[d1, prev_chunk_y], ch="rt2")
            ylast = None
            for bk in range(4):
                ndap = nd[bk]
                pitch = ndap.ap[0][0]
                in0e = apm(ndap, 0, [[pitch, 64], [256, 2], [1, 128]])
                in0o = apm(ndap, 128, [[pitch, 64], [256, 2], [1, 128]])
                in1e = AP(rden_bc, 512 * bk, [[2048, 64], [256, 2], [1, 128]])
                in1o = AP(rden_bc, 64 * 2048 + 512 * bk + 128, [[2048, 64], [256, 2], [1, 128]])
                oute = AP(y_fm, 512 * (2 * bk) + 128 * cc, [[4096, 64], [512, 2], [1, 128]])
                outo = AP(y_fm, 64 * 4096 + 512 * (2 * bk) + 128 * cc, [[4096, 64], [512, 2], [1, 128]])
                S.emit("vector", lambda e, a=oute, b=in0e, c_=in1e: e.tensor_tensor(
                    out=a, in0=b, in1=c_, op=ALU.mult), wait=[d2, pe_last], inc="ve")
                ylast = S.emit("vector", lambda e, a=outo, b=in0o, c_=in1o: e.tensor_tensor(
                    out=a, in0=b, in1=c_, op=ALU.mult), wait=[], inc="ve")
            prev_chunk_y = ylast
        evs = None
        for fc in range(8):
            ping = ps_a if fc % 2 == 0 else ps_b
            def f_pr(e, fc=fc, ping=ping):
                for kc in range(8):
                    ins_ = e.matmul(out=ping[:], lhsT=wTp[kc][:, fc * 128: (fc + 1) * 128],
                             rhs=y_fm[:, kc * 512: (kc + 1) * 512],
                             start=kc == 0, stop=kc == 7)
                return ins_
            m = S.emit("tensor", f_pr, wait=[prev_chunk_y,
                       evs if fc >= 2 else None], inc="pe")
            evs = S.emit("scalar", lambda e, fc=fc, ping=ping: e.copy(
                out=ao_st[:, fc * 512: (fc + 1) * 512], in_=ping[:]),
                wait=[m, prev_proj if fc == 0 else None], inc="ac")
        prev_proj = S.dma("sync", fm_write(ao_sc, OWN, u * 512, 512),
                          ao_st[:].rearrange("p (c t) -> p c t", c=8), wait=[evs],
                          ch="aos")
    S.barrier()

    def stage_fa(l, in_b16, in_n, in_c0, res_scr, res_n, res_f32, res_c0,
                 out_f32, out_b16, out_n, nq, tab_off, mk_base):
        AR.reset()
        wqk = [AR.at([128, 1536], BF16) for _ in range(8)]
        wos = [AR.at([128, 1024], BF16) for _ in range(4)]
        wstg = [AR.at([128, 1536], F32) for _ in range(2)]
        xst_g = [AR.at([128, 4096], BF16) for _ in range(2)]
        kr_buf = [AR.at([128, 1024], BF16) for _ in range(4)]
        v_sl = [AR.at([128, 512], BF16) for _ in range(8)]
        q4 = AR.at([128, 2048], BF16)
        k4 = AR.at([128, 2048], BF16)
        sq4 = AR.at([128, 2048], BF16)
        ss32 = AR.at([128, 32], F32)
        ri32 = AR.at([128, 32], F32)
        rq4 = AR.at([128, 2048], BF16)
        qhs4 = AR.at([128, 2048], BF16)
        t14 = AR.at([128, 2048], BF16)
        t24 = AR.at([128, 2048], BF16)
        qr4 = AR.at([128, 2048], BF16)
        kr4 = AR.at([128, 2048], BF16)
        qr_fm4 = AR.at([128, 2048], BF16)
        qsd = AR.at([128, 64], F32)
        ksd = AR.at([128, 64], F32)
        tab4 = {n: [AR.at([128, 256], F32) for _ in range(2)] for n in tabs}
        mkt = [AR.at([128, 2560], BF16) for _ in range(2)]
        nmax = [AR.at([128, 4], F32) for _ in range(2)]
        pbf = [AR.at([128, 640], BF16) for _ in range(2)]
        den = [AR.at([128, 2], F32) for _ in range(2)]
        rdn = [AR.at([128, 1], F32) for _ in range(2)]
        pT_sb = [AR.at([128, 640], BF16) for _ in range(2)]
        den_g = AR.at([128, 32], F32)
        rdn_g = AR.at([128, 32], F32)
        rdn_t = AR.at([32, 128], BF16)
        rden_sb = [AR.at([128, 512], BF16) for _ in range(4)]
        o_sb = [AR.at([128, 512], BF16) for _ in range(4)]
        res_tf = AR.at([128, 4096], F32 if res_f32 else BF16)
        x1o = AR.at([128, 4096], F32)
        x1ob = AR.at([128, 4096], BF16)

        tprev = None
        for fc in range(8):
            ld = S.dma("sync", wstg[fc % 2][:], AP(wqkv, l * D * 1536 + fc * 128 * 1536,
                       [[1536, 128], [1, 1536]]), wait=[tprev])
            tprev = S.emit("vector", lambda e, fc=fc, s=fc % 2: e.tensor_copy(
                out=wqk[fc][:], in_=wstg[s][:]), wait=[ld], inc="ve")
        for kc in range(4):
            ld = S.dma("sync", wstg[kc % 2][:, 0:1024], AP(wo, l * 512 * D + kc * 128 * D,
                       [[D, 128], [1, D]]), wait=[tprev])
            tprev = S.emit("vector", lambda e, kc=kc, s=kc % 2: e.tensor_copy(
                out=wos[kc][:], in_=wstg[s][:, 0:1024]), wait=[ld], inc="ve")
        S.dma("sync", qsd[:], AP(qsr, l * 64, [[0, 128], [1, 64]]), ch="misc")
        t_ks = S.dma("sync", ksd[:], AP(ksr, l * 64, [[0, 128], [1, 64]]), ch="misc")

        ntile = NHIST + nq
        prev_a3 = (None, None)
        prev_mpt = (None, None)
        prev_dr = (None, None)
        prev_tin = None
        prev_wo = tprev
        prev_x1 = t_ks
        prev_x1f = None
        prev_heads_done = None
        g0 = 0
        while g0 < ntile:
            gtiles = list(range(g0, min(g0 + 4, ntile)))
            gw = len(gtiles)
            gt = gw * 128
            gci = (g0 // 4) % 2
            is_hist = (g0 == 0)
            chs = f"fa{l}_{gci}"
            cin = in_c0 + g0 * 128
            l1 = S.dma("sync", AP(xst_g[gci], 0, [[4096, 128], [512, 4], [1, gt]]),
                       fm_read(in_b16, in_n, cin, gt, nch=4, ch0=0),
                       wait=[prev_wo, prev_x1], ch=chs)
            l2 = S.dma("sync", AP(xst_g[gci], 4 * 512, [[4096, 128], [512, 4], [1, gt]]),
                       fm_read(in_b16, in_n, cin - 1, gt, nch=4, ch0=4), ch=chs)
            tile_ld = l2
            for n in tabs:
                tile_ld = S.dma("sync", AP(tab4[n][gci], 0, [[256, 128], [64, gw], [1, 64]]),
                          AP(tabs[n], (tab_off + g0 * 128) * 64,
                             [[64, 128], [128 * 64, gw], [1, 64]]), ch=chs)
            if not is_hist:
                tile_ld = S.dma("sync", AP(mkt[gci], 0, [[2560, 128], [640, gw], [1, 640]]),
                           AP(mks, (mk_base + g0 - NHIST) * 128 * 640,
                              [[640, 128], [128 * 640, gw], [1, 640]]), ch=chs)
            ev_q, ev_k, ev_v = {}, {}, {}
            for ti, st_ in enumerate(gtiles):
                psq = (ps_a if ti % 2 == 0 else ps_b)[:]
                psk = (ps_c if ti % 2 == 0 else ps_d)[:]
                psv = ps_E2[:, (ti % 2) * 512:(ti % 2) * 512 + 512]
                wq_ = [tile_ld if ti == 0 else None,
                       prev_x1 if ti == 0 else None]
                if ti == 0:
                    wq_ += [prev_a3[0], prev_a3[1], prev_heads_done]
                if ti >= 2:
                    wq_.append(ev_v[ti - 2])
                def f_qkv(e, gci=gci, ti=ti, psq=psq, psk=psk, psv=psv):
                    for fc in range(8):
                        lhs = xst_g[gci][:, fc * 512 + ti * 128: fc * 512 + ti * 128 + 128]
                        ins_ = e.matmul(out=psq, lhsT=lhs, rhs=wqk[fc][:, 0:512],
                                 start=fc == 0, stop=fc == 7)
                        ins_ = e.matmul(out=psk, lhsT=lhs, rhs=wqk[fc][:, 512:1024],
                                 start=fc == 0, stop=fc == 7)
                        ins_ = e.matmul(out=psv, lhsT=lhs, rhs=wqk[fc][:, 1024:1536],
                                 start=fc == 0, stop=fc == 7)
                    return ins_
                m = S.emit("tensor", f_qkv, wait=wq_, inc="pe")
                ev_q[ti] = S.emit("scalar", lambda e, ti=ti, psq=psq: e.copy(
                    out=q4[:, ti * 512: ti * 512 + 512], in_=psq), wait=[m], inc="ac")
                ev_k[ti] = S.emit("scalar", lambda e, ti=ti, psk=psk: e.copy(
                    out=k4[:, ti * 512: ti * 512 + 512], in_=psk), wait=[], inc="ac")
                ev_v[ti] = S.emit("scalar", lambda e, s=st_, psv=psv: e.copy(
                    out=v_sl[s % 8][:], in_=psv), wait=[], inc="ac")
            def chain(src4, dst4, sd, cs, sn, ev_last, cols0, xw2=None,
                      xw5=None, gt=gt, gw=gw, gci=gci):
                c2 = S.emit("scalar", lambda e: e.activation(out=sq4[:, 0:gt * 4],
                            in_=src4[:, 0:gt * 4], func=AF.Square),
                            wait=[ev_last, xw2], inc="ac")
                c3 = S.emit("vector", lambda e: e.reduce_sum(
                    out=ss32[:, cols0:cols0 + gw * 8],
                    in_=AP(sq4, 0, [[2048, 128], [64, gw * 8], [1, 64]]),
                    axis=AX.X), wait=[c2], inc="ve")
                c4a = S.emit("scalar", lambda e: e.activation(
                    out=ri32[:, cols0:cols0 + gw * 8], in_=ss32[:, cols0:cols0 + gw * 8],
                    func=AF.Ln, bias=cst[:, 2:3]), wait=[c3], inc="ac")
                c4 = S.emit("scalar", lambda e: e.activation(
                    out=ri32[:, cols0:cols0 + gw * 8], in_=ri32[:, cols0:cols0 + gw * 8],
                    func=AF.Exp, scale=cst[:, 3:4]), wait=[c4a], inc="ac")
                c5 = S.emit("vector", lambda e, sd=sd: e.tensor_tensor(
                    out=AP(rq4, 0, [[2048, 128], [512, gw], [64, 8], [1, 64]]),
                    in0=AP(ri32, cols0, [[32, 128], [8, gw], [1, 8], [0, 64]]),
                    in1=AP(sd, 0, [[64, 128], [0, gw], [0, 8], [1, 64]]),
                    op=ALU.mult), wait=[c4, xw5], inc="ve")
                c6 = S.emit("vector", lambda e, src4=src4: e.tensor_tensor(
                    out=qhs4[:, 0:gt * 4], in0=src4[:, 0:gt * 4], in1=rq4[:, 0:gt * 4],
                    op=ALU.mult), wait=[c5], inc="ve")
                c7 = S.emit("vector", lambda e, cs=cs: e.tensor_tensor(
                    out=AP(t14, 0, [[2048, 128], [512, gw], [64, 8], [1, 64]]),
                    in0=AP(qhs4, 0, [[2048, 128], [512, gw], [64, 8], [1, 64]]),
                    in1=AP(tab4[cs][gci], 0, [[256, 128], [64, gw], [0, 8], [1, 64]]),
                    op=ALU.mult), wait=[c6], inc="ve")
                c8 = S.emit("vector", lambda e, sn=sn: e.tensor_tensor(
                    out=AP(t24, 0, [[2048, 128], [512, gw], [64, 8], [32, 2], [1, 32]]),
                    in0=AP(qhs4, 32, [[2048, 128], [512, gw], [64, 8], [-32, 2], [1, 32]]),
                    in1=AP(tab4[sn][gci], 0, [[256, 128], [64, gw], [0, 8], [32, 2], [1, 32]]),
                    op=ALU.mult), wait=[c7], inc="ve")
                c9 = S.emit("vector", lambda e, dst4=dst4: e.tensor_tensor(
                    out=dst4[:, 0:gt * 4], in0=t14[:, 0:gt * 4], in1=t24[:, 0:gt * 4],
                    op=ALU.add), wait=[c8], inc="ve")
                return c9, c3, c6
            cq, qc3, qc6 = chain(q4, qr4, qsd, "csq", "ssq", ev_q[gw - 1], 0)
            ck, _, _ = chain(k4, kr4, ksd, "csk", "ssk", ev_k[gw - 1], 0,
                             xw2=qc3, xw5=qc6)
            ek_tok = {}
            pview = ps_G2[:].bitcast(BF16)[:, 0:1024]
            for ti, st_ in enumerate(gtiles):
                def f_tq(e, ti=ti):
                    for jj in range(4):
                        ins_ = e.transpose(out=pview[:, jj * 128: (jj + 1) * 128],
                                    in_=qr4[:, ti * 512 + jj * 128: ti * 512 + (jj + 1) * 128],
                                    identity=idb[:])
                        ins_ = e.transpose(out=pview[:, 512 + jj * 128: 512 + (jj + 1) * 128],
                                    in_=kr4[:, ti * 512 + jj * 128: ti * 512 + (jj + 1) * 128],
                                    identity=idb[:])
                    return ins_
                mt = S.emit("tensor", f_tq, wait=[cq, ck] if ti == 0 else
                            [ek_tok[ti - 1]], inc="pe")
                e1 = S.emit("scalar", lambda e, ti=ti: e.copy(
                    out=qr_fm4[:, ti * 512: (ti + 1) * 512], in_=pview[:, 0:512]),
                    wait=[mt], inc="ac")
                kcol = 512 + (st_ % 4) * 128
                def f_ek(e, kcol=kcol):
                    for p_ in range(4):
                        ins_ = e.copy(out=kr_buf[p_][:, kcol: kcol + 128],
                               in_=pview[:, 512 + p_ * 128: 512 + (p_ + 1) * 128])
                    return ins_
                ek_tok[ti] = S.emit("scalar", f_ek, wait=[e1], inc="ac")
            front_done = ek_tok[gw - 1]
            if is_hist:
                def f_slide0(e):
                    for p_ in range(4):
                        ins_ = e.tensor_copy(out=kr_buf[p_][:, 0:512],
                                             in_=kr_buf[p_][:, 512:1024])
                    return ins_
                prev_wo = S.emit("vector", f_slide0, wait=[front_done], inc="ve")
                g0 += 4
                continue
            o_done = []
            for ti, st_ in enumerate(gtiles):
                cc = ti
                avlast = None
                a3tok = [None] * 8
                a5tok = [None] * 8
                drtok = [None] * 8
                mpt_tok = [None] * 8

                def emit_pt(hh):
                    ptp = (ps_a if hh % 2 == 0 else ps_b)[:].bitcast(BF16)
                    def f_pt(e, ptp=ptp, ii=hh % 2):
                        for jj in range(5):
                            ins_ = e.transpose(out=ptp[:, jj * 128: (jj + 1) * 128],
                                        in_=pbf[ii][:, jj * 128: (jj + 1) * 128],
                                        identity=idb[:])
                        return ins_
                    mpt = S.emit("tensor", f_pt, wait=[a5tok[hh]], inc="pe")
                    mpt_tok[hh] = mpt
                    a6 = S.emit("scalar", lambda e, ptp=ptp, ii=hh % 2: e.copy(
                                out=pT_sb[ii][:], in_=ptp[:, 0:640]),
                                wait=[mpt], inc="ac")
                    return a6

                def emit_av(hh, a6, ti=ti, st_=st_):
                    def f_av(e, hh=hh, s=st_):
                        for jj in range(5):
                            vsl = v_sl[(s - 4 + jj) % 8]
                            ins_ = e.matmul(out=ps_c[64 * (hh % 2):64 * (hh % 2) + 64,
                                             128 * (hh // 2): 128 * (hh // 2) + 128],
                                     lhsT=vsl[:, 64 * hh: 64 * hh + 64],
                                     rhs=pT_sb[hh % 2][:, jj * 128: (jj + 1) * 128],
                                     start=(hh <= 1 and jj == 0),
                                     stop=(hh >= 6 and jj == 4),
                                     skip_group_check=(hh % 2 == 1))
                        return ins_
                    return S.emit("tensor", f_av, wait=[a6, ev_v[ti]], inc="pe")

                a6_pend = None
                for hh in range(8):
                    pj, par, ii = hh // 2, 64 * (hh % 2), hh % 2
                    if hh >= 2:
                        a6_pend = emit_pt(hh - 2)
                    sim2 = ps_E2 if ii == 0 else ps_G2
                    wsim = []
                    if hh <= 1 and ti == 0:
                        wsim.append(front_done)
                    if hh >= 2:
                        wsim.append(a3tok[hh - 2])
                    elif prev_a3[ii] is not None:
                        wsim.append(prev_a3[ii])
                    def f_sim(e, pj=pj, par=par, sim2=sim2, cc=cc, ti=ti):
                        lhs = qr_fm4[par:par + 64, ti * 512 + pj * 128: ti * 512 + (pj + 1) * 128]
                        ins_ = e.matmul(out=sim2[:, 0:512], lhsT=lhs,
                                 rhs=kr_buf[pj][par:par + 64, cc * 128: cc * 128 + 512],
                                 start=True, stop=True)
                        ins_ = e.matmul(out=sim2[:, 512:640], lhsT=lhs,
                                 rhs=kr_buf[pj][par:par + 64, cc * 128 + 512: cc * 128 + 640],
                                 start=True, stop=True)
                        return ins_
                    msim = S.emit("tensor", f_sim, wait=wsim, inc="pe")
                    x1_ = S.emit("scalar", lambda e, sim2=sim2, ii=ii: e.activation(
                        out=pbf[ii][:, 0:640], in_=sim2[:, 0:640], func=AF.Exp),
                        wait=[msim, mpt_tok[hh - 2] if hh >= 2 else None,
                              drtok[hh - 2] if hh >= 2 else None,
                              prev_mpt[ii] if hh < 2 else None,
                              prev_dr[ii] if hh < 2 else None], inc="ac")
                    a3tok[hh] = x1_
                    m1 = S.emit("vector", lambda e, ii=ii, gci=gci, ti=ti:
                        e.tensor_tensor(out=pbf[ii][:], in0=pbf[ii][:],
                                        in1=mkt[gci][:, ti * 640: ti * 640 + 640],
                                        op=ALU.mult), wait=[x1_], inc="ve")
                    drtok[hh] = S.emit("vector", lambda e, ii=ii, col=ti * 8 + hh:
                        e.reduce_sum(out=den_g[:, col:col + 1],
                                     in_=pbf[ii][:, 0:640], axis=AX.X),
                        wait=[m1], inc="ve")
                    a5tok[hh] = m1
                    if hh >= 2:
                        avlast = emit_av(hh - 2, a6_pend)
                a6p = emit_pt(6)
                a7p = emit_pt(7)
                avlast = emit_av(6, a6p)
                avlast = emit_av(7, a7p)
                prev_a3 = (a3tok[6], a3tok[7])
                prev_mpt = (mpt_tok[6], mpt_tok[7])
                prev_dr = (drtok[6], drtok[7])
                def f_eo(e, cc=cc):
                    for p_ in range(4):
                        ins_ = e.copy(out=o_sb[p_][:, cc * 128: (cc + 1) * 128],
                               in_=ps_c[:, 128 * p_: 128 * (p_ + 1)])
                    return ins_
                prev_wo = S.emit("scalar", f_eo, wait=[avlast], inc="ac")
                o_done.append(prev_wo)
            prev_heads_done = prev_wo
            rc = S.emit("vector", lambda e, gw=gw: e.reciprocal(
                out=rdn_g[:, 0:gw * 8], in_=den_g[:, 0:gw * 8]), wait=[], inc="ve")
            mtd = S.emit("tensor", lambda e, gw=gw: e.transpose(
                out=ps_d[0:gw * 8, 0:128],
                in_=rdn_g[:, 0:gw * 8], identity=idf[:]), wait=[rc], inc="pe")
            aev = S.emit("scalar", lambda e, gw=gw: e.copy(
                out=rdn_t[0:gw * 8, :],
                in_=ps_d[0:gw * 8, 0:128]), wait=[mtd], inc="ac")
            dch = f"dn{l}"
            o1 = S.dma("sync", AP(dn_sc, 0, [[128, gw * 8], [1, 128]]),
                       rdn_t[0:gw * 8, :], wait=[aev, prev_tin], ch=dch)
            tin = None
            for p_ in range(4):
                for half in range(2):
                    hq = 2 * p_ + half
                    base = rden_sb[p_][64 * half:64 * half + 64, 0:512]
                    tin = S.dma("sync", apm(base, 0, [[512, 64], [128, gw], [1, 128]]),
                                AP(dn_sc, hq * 128, [[0, 64], [8 * 128, gw], [1, 128]]),
                                wait=[o1], ch=dch)
            prev_tin = tin
            ttn = None
            for p_ in range(4):
                ttn = S.emit("vector", lambda e, p_=p_, nt2=gw * 128: e.tensor_tensor(
                    out=o_sb[p_][:, 0:nt2], in0=o_sb[p_][:, 0:nt2],
                    in1=rden_sb[p_][:, 0:nt2], op=ALU.mult),
                    wait=[tin if p_ == 0 else None,
                          o_done[-1] if p_ == 0 else None], inc="ve")
            n_t = gt
            lres = S.dma("sync", AP(res_tf, 0, [[4096, 128], [n_t, 8], [1, n_t]]),
                         fm_read(res_scr, res_n, res_c0 + (g0 - NHIST) * 128, n_t),
                         wait=[o_done[-1]], ch="lres")
            x1done = []
            for fc in range(8):
                ping = ps_E2[:, (fc % 2) * 512:(fc % 2) * 512 + 512]
                def f_wo(e, fc=fc, ping=ping, n_t=n_t):
                    for p_ in range(4):
                        ins_ = e.matmul(out=ping[:, 0:n_t],
                                 lhsT=wos[p_][:, fc * 128: (fc + 1) * 128],
                                 rhs=o_sb[p_][:, 0:n_t],
                                 start=p_ == 0, stop=p_ == 3)
                    return ins_
                m = S.emit("tensor", f_wo, wait=[ttn, lres,
                           x1done[-2] if len(x1done) >= 2 else None], inc="pe")
                x1done.append(S.emit("vector", lambda e, fc=fc, ping=ping, n_t=n_t:
                    e.tensor_add(out=AP(x1o, fc * 512, [[4096, 128], [1, n_t]]),
                                 in0=ping[:, 0:n_t],
                                 in1=AP(res_tf, fc * n_t, [[4096, 128], [1, n_t]])),
                    wait=[m], inc="ve"))
            cb = S.emit("gpsimd", lambda e, n_t=n_t: e.tensor_copy(
                out=AP(x1ob, 0, [[4096, 128], [512, 8], [1, n_t]]),
                in_=AP(x1o, 0, [[4096, 128], [512, 8], [1, n_t]])),
                wait=x1done, inc="gp")
            oc0 = (g0 - NHIST) * 128
            s1 = S.dma("sync", fm_write(out_f32, out_n, oc0, n_t),
                       AP(x1o, 0, [[4096, 128], [512, 8], [1, n_t]]), wait=x1done,
                       ch="fs1")
            s2 = S.dma("sync", fm_write(out_b16, out_n, oc0, n_t),
                       AP(x1ob, 0, [[4096, 128], [512, 8], [1, n_t]]), wait=[cb],
                       ch="fs2")
            prev_x1 = s2
            prev_x1f = s1
            if g0 + 4 < ntile:
                def f_slide(e):
                    for p_ in range(4):
                        ins_ = e.tensor_copy(out=kr_buf[p_][:, 0:512],
                                             in_=kr_buf[p_][:, 512:1024])
                    return ins_
                prev_wo = S.emit("vector", f_slide,
                                 wait=[prev_wo, prev_x1, prev_x1f], inc="ve")
            g0 += 4
        S.barrier()

    def stage_fb(l, in_f32, in_b16, in_n, in_c0, nq, out_f32, out_b16, out_n,
                 assemble=False):
        AR.reset()
        w1s = [AR.at([128, W1C], BF16) for _ in range(8)]
        w2s = [AR.at([128, 1024], BF16) for _ in range(NKC2)]
        wst_off = AR.cur
        wstg = AR.at([128, W1C], F32)
        wraw_off = AR.cur
        wraw = AR.at([128, W1C], BF16)
        sq2 = AR.at([128, 512], BF16)
        strow = AR.at([128, 1024], F32)
        glu0_off = AR.cur
        glu_sb = [AR.at([128, 512], BF16) for _ in range(NKC2)]
        lgc = AR.at([128, 8], F32)
        lbc = AR.at([128, 8], BF16)
        b2c = AR.at([128, 8], F32)
        b1es = AR.at([128, 44], F32)
        b1pp = AR.at([128, 44], F32)
        acbc = AR.at([128, 1024], F32)
        res2 = [AR.at([128, 512], F32) for _ in range(2)]
        gact2 = [AR.at([128, 512], BF16) for _ in range(2)]
        xs1 = nc.alloc_sbuf_tensor_at(f"xs1_{l}", [128, 4096], BF16, offset=wst_off)
        h_sb = nc.alloc_sbuf_tensor_at(f"hsb_{l}", [128, 4096], BF16,
                                       offset=wst_off + 8192)
        x2st = nc.alloc_sbuf_tensor_at(f"x2st{l}", [128, 4096], F32, offset=wraw_off)
        if assemble:
            att_t = [nc.alloc_sbuf_tensor_at(f"att{l}_{i}", [128, 512], BF16,
                     offset=wst_off + 16384 + i * 1024) for i in range(2)]
            xab = [nc.alloc_sbuf_tensor_at(f"xab{l}_{i}", [128, 1024], F32,
                   offset=wst_off + 8192 + i * 4096) for i in range(2)]

        S.dma("sync", lgc[:], AP(lgv, l * 128 * 8, [[8, 128], [1, 8]]), ch="misc")
        S.dma("sync", lbc[:], AP(lbv, l * 128 * 8, [[8, 128], [1, 8]]), ch="misc")
        S.dma("sync", b2c[:], AP(b2v, l * 128 * 8, [[8, 128], [1, 8]]), ch="misc")
        t0 = S.dma("sync", b1pp[:], AP(b1p, l * 128 * 44, [[44, 128], [1, 44]]),
                   ch="misc")
        tprev = None
        pe_b1 = None
        for kc in range(8):
            ld = S.dma("sync", wstg[:], AP(w1p, l * D * W1C + kc * 128 * W1C,
                       [[W1C, 128], [1, W1C]]),
                       wait=[tprev, t0 if kc == 0 else None])
            c1 = S.emit("vector", lambda e: e.tensor_copy(out=wraw[:],
                        in_=wstg[:]), wait=[ld, pe_b1], inc="ve")
            def f_b1(e, kc=kc):
                for oc in range(44):
                    ins_ = e.matmul(out=ps_c[0:128, oc:oc + 1],
                             lhsT=wraw[:, oc * 128: (oc + 1) * 128],
                             rhs=lbc[:, kc:kc + 1],
                             start=(kc == 0 and oc == 0),
                             stop=(kc == 7 and oc == 43))
                return ins_
            pe_b1 = S.emit("tensor", f_b1, wait=[c1], inc="pe")
            tprev = S.emit("vector", lambda e, kc=kc: e.tensor_scalar_mul(
                out=w1s[kc][:], in0=wstg[:], scalar1=lgc[:, kc:kc + 1]),
                wait=[c1], inc="ve")
        tb1 = S.emit("vector", lambda e: e.tensor_add(out=b1es[:], in0=ps_c[:, 0:44],
                     in1=b1pp[:]), wait=[pe_b1, tprev], inc="ve")
        w2cast = []
        for kc in range(NKC2):
            half = (kc % 2) * 1024
            ld = S.dma("sync", wstg[:, half: half + 1024],
                       AP(w2p, l * HIDP * D + kc * 128 * D, [[D, 128], [1, D]]),
                       wait=[w2cast[kc - 2] if kc >= 2 else tprev], ch=f"wc{kc % 2}")
            w2cast.append(S.emit("vector", lambda e, kc=kc, half=half: e.tensor_copy(
                out=w2s[kc][:], in_=wstg[:, half: half + 1024]), wait=[ld], inc="ve"))
        tprev = w2cast[-1]

        prev_st = tb1
        prev_grp = tprev
        prev_la = tb1
        prev_s1 = None
        g0 = 0
        while g0 < nq:
            gw = min(4, nq - g0)
            n_t = gw * 128
            cin = in_c0 + g0 * 128
            l1 = S.dma("sync", AP(xs1, 0, [[4096, 128], [512, 4], [1, n_t]]),
                       fm_read(in_b16, in_n, cin, n_t, nch=4, ch0=0),
                       wait=[prev_grp, prev_s1], ch="fbin")
            l2 = S.dma("sync", AP(xs1, 4 * 512, [[4096, 128], [512, 4], [1, n_t]]),
                       fm_read(in_b16, in_n, cin - 1, n_t, nch=4, ch0=4), ch="fbin")
            pe_st = None
            sq_prev = None
            for fc in range(8):
                sq_prev = S.emit("scalar", lambda e, fc=fc, n_t=n_t: e.activation(
                    out=sq2[:, 0:n_t], in_=xs1[:, fc * 512: fc * 512 + n_t],
                    func=AF.Square),
                    wait=[l2] if fc == 0 else [pe_st], inc="ac")
                def f_stm(e, fc=fc, n_t=n_t):
                    ins_ = e.matmul(out=ps_G2[0:1, 0:n_t], lhsT=ones1[:],
                             rhs=xs1[:, fc * 512: fc * 512 + n_t],
                             start=fc == 0, stop=fc == 7)
                    ins_ = e.matmul(out=ps_G2[0:1, 512:512 + n_t], lhsT=ones1[:],
                             rhs=sq2[:, 0:n_t], start=fc == 0, stop=fc == 7)
                    return ins_
                pe_st = S.emit("tensor", f_stm,
                               wait=[sq_prev, prev_st if fc == 0 else None], inc="pe")
            v1 = S.emit("vector", lambda e, n_t=n_t: e.tensor_scalar_mul(
                out=strow[0:1, 0:n_t], in0=ps_G2[0:1, 0:n_t], scalar1=1.0 / 1024),
                wait=[pe_st], inc="ve")
            v2 = S.emit("vector", lambda e, n_t=n_t: e.tensor_scalar_mul(
                out=strow[0:1, 512:512 + n_t], in0=ps_G2[0:1, 512:512 + n_t],
                scalar1=1.0 / 1024), wait=[v1], inc="ve")
            v3 = S.emit("vector", lambda e, n_t=n_t: e.tensor_mul(
                out=acbc[0:1, 0:n_t], in0=strow[0:1, 0:n_t], in1=strow[0:1, 0:n_t]),
                wait=[v2], inc="ve")
            v4 = S.emit("vector", lambda e, n_t=n_t: e.tensor_sub(
                out=strow[0:1, 512:512 + n_t], in0=strow[0:1, 512:512 + n_t],
                in1=acbc[0:1, 0:n_t]), wait=[v3], inc="ve")
            v5a = S.emit("scalar", lambda e, n_t=n_t: e.activation(
                out=strow[0:1, 512:512 + n_t], in_=strow[0:1, 512:512 + n_t],
                func=AF.Ln, bias=cst[0:1, 0:1]), wait=[v4], inc="ac")
            v5 = S.emit("scalar", lambda e, n_t=n_t: e.activation(
                out=strow[0:1, 512:512 + n_t], in_=strow[0:1, 512:512 + n_t],
                func=AF.Exp, scale=cst[0:1, 3:4]), wait=[v5a], inc="ac")
            v6 = S.emit("vector", lambda e, n_t=n_t: e.tensor_mul(
                out=strow[0:1, 0:n_t], in0=strow[0:1, 0:n_t],
                in1=strow[0:1, 512:512 + n_t]), wait=[v5], inc="ve")
            v7 = S.emit("vector", lambda e, n_t=n_t: e.tensor_scalar_mul(
                out=strow[0:1, 0:n_t], in0=strow[0:1, 0:n_t], scalar1=-1.0),
                wait=[v6], inc="ve")
            d1 = S.dma("scalar", AP(ac_sc, 0, [[1024, 1], [512, 2], [1, n_t]]),
                       AP(strow, 0, [[1024, 1], [512, 2], [1, n_t]]),
                       wait=[v7], ch="acs")
            d2 = S.dma("sync", AP(acbc, 0, [[1024, 128], [512, 2], [1, n_t]]),
                       AP(ac_sc, 0, [[0, 128], [512, 2], [1, n_t]]),
                       wait=[d1], ch="acr")
            hdone = []
            for fc in range(8):
                h1 = S.emit("vector", lambda e, fc=fc, n_t=n_t: e.tensor_mul(
                    out=h_sb[:, fc * 512: fc * 512 + n_t],
                    in0=xs1[:, fc * 512: fc * 512 + n_t],
                    in1=acbc[:, 512:512 + n_t]),
                    wait=[d2] if fc == 0 else [], inc="ve")
                hdone.append(S.emit("vector", lambda e, fc=fc, n_t=n_t: e.tensor_add(
                    out=h_sb[:, fc * 512: fc * 512 + n_t],
                    in0=h_sb[:, fc * 512: fc * 512 + n_t],
                    in1=acbc[:, 0:n_t]), wait=[h1], inc="ve"))
            glu_done = []
            for jj in range(NKC2):
                psA = ps_a if jj % 2 == 0 else ps_c
                psG = ps_b if jj % 2 == 0 else ps_d
                def f_w1(e, jj=jj, psA=psA, psG=psG, n_t=n_t):
                    for kc in range(8):
                        rhs = h_sb[:, kc * 512: kc * 512 + n_t]
                        ins_ = e.matmul(out=psA[:, 0:n_t],
                                 lhsT=w1s[kc][:, (2 * jj) * 128:(2 * jj + 1) * 128],
                                 rhs=rhs, start=kc == 0, stop=kc == 7)
                        ins_ = e.matmul(out=psG[:, 0:n_t],
                                 lhsT=w1s[kc][:, (2 * jj + 1) * 128:(2 * jj + 2) * 128],
                                 rhs=rhs, start=kc == 0, stop=kc == 7)
                    return ins_
                m = S.emit("tensor", f_w1, wait=[hdone[-1],
                           glu_done[-2] if len(glu_done) >= 2 else None], inc="pe")
                ga = gact2[jj % 2]
                g1 = S.emit("scalar", lambda e, jj=jj, psG=psG, n_t=n_t, ga=ga: e.activation(
                    out=ga[:, 0:n_t], in_=psG[:, 0:n_t],
                    func=AF.Gelu, bias=b1es[:, 2 * jj + 1: 2 * jj + 2]), wait=[m], inc="ac")
                glu_done.append(S.emit("vector", lambda e, jj=jj, psA=psA, n_t=n_t, ga=ga:
                    e.scalar_tensor_tensor(out=glu_sb[jj][:, 0:n_t],
                        in0=psA[:, 0:n_t], scalar=b1es[:, 2 * jj: 2 * jj + 1],
                        in1=ga[:, 0:n_t], op0=ALU.add, op1=ALU.mult),
                    wait=[g1], inc="ve"))
            x2done = []
            asmdone = []
            for fc in range(8):
                ping = ps_G2[:, (fc % 2) * 512:(fc % 2) * 512 + 512]
                lr = S.dma("sync", res2[fc % 2][:, 0:n_t],
                           AP(in_f32, fc * 128 * in_n + cin, [[in_n, 128], [1, n_t]]),
                           wait=[x2done[-2] if len(x2done) >= 2 else glu_done[1]],
                           ch=f"lr{fc % 2}")
                def f_w2b(e, fc=fc, ping=ping, n_t=n_t):
                    for kc in range(NKC2):
                        ins_ = e.matmul(out=ping[:, 0:n_t],
                                 lhsT=w2s[kc][:, fc * 128: (fc + 1) * 128],
                                 rhs=glu_sb[kc][:, 0:n_t],
                                 start=kc == 0, stop=kc == NKC2 - 1)
                    return ins_
                m = S.emit("tensor", f_w2b, wait=[glu_done[-1],
                           x2done[-2] if len(x2done) >= 2 else None], inc="pe")
                x2done.append(S.emit("vector", lambda e, fc=fc, ping=ping, n_t=n_t:
                    e.scalar_tensor_tensor(
                        out=AP(x2st, fc * 512, [[4096, 128], [1, n_t]]),
                        in0=ping[:, 0:n_t], scalar=b2c[:, fc:fc + 1],
                        in1=res2[fc % 2][:, 0:n_t], op0=ALU.add, op1=ALU.add),
                    wait=[m, lr], inc="ve"))
                if assemble:
                    la = S.dma("sync", att_t[fc % 2][:, 0:n_t],
                               AP(ao_sc, fc * 128 * OWN + g0 * 128, [[OWN, 128], [1, n_t]]),
                               wait=[asmdone[-2] if len(asmdone) >= 2 else prev_la],
                               ch=f"la{fc % 2}")
                    asmdone.append(S.emit("vector", lambda e, fc=fc, n_t=n_t:
                        e.tensor_add(
                            out=AP(x2st, fc * 512, [[4096, 128], [1, n_t]]),
                            in0=AP(x2st, fc * 512, [[4096, 128], [1, n_t]]),
                            in1=att_t[fc % 2][:, 0:n_t]),
                        wait=[x2done[-1], la], inc="ve"))
            if not assemble:
                def f_cb(e, n_t=n_t):
                    return e.tensor_copy(
                        out=AP(h_sb, 0, [[4096, 128], [512, 8], [1, n_t]]),
                        in_=AP(x2st, 0, [[4096, 128], [512, 8], [1, n_t]]))
                cb = S.emit("gpsimd", f_cb, wait=x2done, inc="gp")
                s1 = S.dma("sync", fm_write(out_f32, out_n, g0 * 128, n_t),
                           AP(x2st, 0, [[4096, 128], [512, 8], [1, n_t]]), wait=x2done,
                           ch="fs1")
                s2 = S.dma("sync", fm_write(out_b16, out_n, g0 * 128, n_t),
                           AP(h_sb, 0, [[4096, 128], [512, 8], [1, n_t]]), wait=[cb],
                           ch="fs2")
                prev_grp = s2
                prev_st = s2
                prev_s1 = s1
            else:
                pa = None
                for tt in range(gw):
                    lx = S.dma("sync", xab[tt % 2][:],
                               AP(x_all, (4096 + g0 * 128 + tt * 128) * D, [[D, 128], [1, D]]),
                               wait=[pa if pa else asmdone[-1]], ch=f"lx{tt % 2}")
                    def f_at(e, tt=tt):
                        for fc in range(8):
                            ins_ = e.transpose(out=ps_E2[:, fc * 128: (fc + 1) * 128],
                                        in_=AP(x2st, fc * 512 + tt * 128, [[4096, 128], [1, 128]]),
                                        identity=idf[:])
                        return ins_
                    m = S.emit("tensor", f_at, wait=[asmdone[-1], pa], inc="pe")
                    av = S.emit("vector", lambda e, tt=tt: e.tensor_add(
                        out=xab[tt % 2][:], in0=ps_E2[:, 0:1024], in1=xab[tt % 2][:]),
                        wait=[m, lx], inc="ve")
                    pa = S.dma("sync", AP(out_t, (g0 + tt) * 128 * D, [[D, 128], [1, D]]),
                               xab[tt % 2][:], wait=[av], ch="out")
                prev_la = pa
                prev_grp = pa
                prev_st = pa
            g0 += gw
        S.barrier()

    if upto >= 3:
        stage_fa(0, xn_sc, XN_N, (FA0_Q0 - 512) - (-4096), xn_sc, XN_N, False,
                 FA0_Q0 - (-4096), x1f_sc, x1b_sc, X1_N, FA0_NQ,
                 (FA0_Q0 - 512) - TAB0, 0)
    if upto >= 4:
        stage_fb(0, x1f_sc, x1b_sc, X1_N, FB0_Q0 - FA0_Q0, FB0_NQ,
                 x2f_sc, x2b_sc, X2_N)
    if upto >= 5:
        stage_fa(1, x2b_sc, X2_N, (FA1_Q0 - 512) - FB0_Q0, x2f_sc, X2_N, True,
                 FA1_Q0 - FB0_Q0, x3f_sc, x3b_sc, X3_N, FA1_NQ,
                 (FA1_Q0 - 512) - TAB0, FA0_NQ)
    if upto >= 6:
        stage_fb(1, x3f_sc, x3b_sc, X3_N, FB1_Q0 - FA1_Q0, FB1_NQ,
                 None, None, None, assemble=True)

    S.q["sync"].append((tuple((n, S.c[n]) for n in sorted(S.dma_chans)
                              if S.c[n] > 0), None, None, False))
    S.realize()
    return nc



def _host_tables():
    hi = np.arange(0, 64, 2, dtype=np.float64)
    inv_freq = 1.0 / 10000.0 ** (hi / 64)
    t = np.arange(T, dtype=np.float64)
    fr = t[:, None] * inv_freq[None, :]
    freqs = np.concatenate([fr, fr], axis=-1)
    base = (hi + 0.4 * 64) / (1.4 * 64)
    power = (t - T // 2) / float(WIN // 2)
    scale = base[None, :] ** power[:, None]
    scale = np.concatenate([scale, scale], axis=-1)
    cos, sin = np.cos(freqs), np.sin(freqs)
    sgn = np.where(np.arange(64) < 32, -1.0, 1.0)[None, :]
    csq = (8.0 * cos * scale).astype(np.float32)
    ssq = (8.0 * sgn * sin * scale).astype(np.float32)
    csk = (cos / scale).astype(np.float32)
    ssk = (sgn * sin / scale).astype(np.float32)
    return csq, ssq, csk, ssk


def _masks_for(t0):
    out = np.zeros((NMASK, 128, 640), dtype=np.float32)
    p = np.arange(128)[:, None]
    j = np.arange(640)[None, :]
    idx = 0
    for (q0, nqq) in ((FA0_Q0, FA0_NQ), (FA1_Q0, FA1_NQ)):
        for s in range(nqq):
            i0 = t0 + q0 + s * 128
            lower = np.maximum(p, 512 - i0)
            empty = lower > p + 512
            lo = np.where(empty, p, lower)
            valid = (j >= lo) & (j <= p + 512)
            out[idx] = np.where(valid, 1.0, 0.0)
            idx += 1
    return out.astype(ml_dtypes.bfloat16)


def _perm_pad_w1(w1, b1):
    w1n = np.zeros((D, W1C), dtype=w1.dtype)
    b1n = np.zeros((W1C,), dtype=b1.dtype)
    for jj in range(NKC2):
        a0, a1 = jj * 128, min((jj + 1) * 128, HID)
        g0_, g1_ = HID + jj * 128, min(HID + (jj + 1) * 128, 2 * HID)
        w1n[:, 2 * jj * 128: 2 * jj * 128 + (a1 - a0)] = w1[:, a0:a1]
        w1n[:, (2 * jj + 1) * 128: (2 * jj + 1) * 128 + (g1_ - g0_)] = w1[:, g0_:g1_]
        b1n[2 * jj * 128: 2 * jj * 128 + (a1 - a0)] = b1[a0:a1]
        b1n[(2 * jj + 1) * 128: (2 * jj + 1) * 128 + (g1_ - g0_)] = b1[g0_:g1_]
    return w1n, b1n


_NC_CACHE = {}


def kernel(L0_attn_x, L0_attn_w_attn, L0_attn_w_proj, L0_ffn_wqkv,
           L0_ffn_q_scale, L0_ffn_k_scale, L0_ffn_wo, L0_ffn_ln_g, L0_ffn_ln_b,
           L0_ffn_w1, L0_ffn_b1, L0_ffn_w2, L0_ffn_b2, _dump=(), _cores=None,
           _upto=6):
    x = np.ascontiguousarray(np.asarray(L0_attn_x, dtype=np.float32))
    csq, ssq, csk, ssk = _host_tables()
    w1pl, b1pl, w2pl = [], [], []
    for l in range(2):
        w1n, b1n = _perm_pad_w1(np.asarray(L0_ffn_w1)[l], np.asarray(L0_ffn_b1)[l])
        w1pl.append(w1n)
        b1pl.append(b1n.reshape(44, 128).T)
        w2n = np.zeros((HIDP, D), dtype=np.float32)
        w2n[:HID] = np.asarray(L0_ffn_w2)[l]
        w2pl.append(w2n)
    w1p_a = np.ascontiguousarray(np.stack(w1pl).astype(np.float32))
    b1p_a = np.ascontiguousarray(np.stack(b1pl).astype(np.float32))
    w2p_a = np.ascontiguousarray(np.stack(w2pl))
    lbv_a = np.ascontiguousarray(np.stack(
        [np.asarray(L0_ffn_ln_b)[l].reshape(8, 128).T for l in range(2)]
    ).astype(ml_dtypes.bfloat16))
    lgv_a = np.ascontiguousarray(np.stack(
        [np.asarray(L0_ffn_ln_g)[l].reshape(8, 128).T for l in range(2)]).astype(np.float32))
    b2v_a = np.ascontiguousarray(np.stack(
        [np.asarray(L0_ffn_b2)[l].reshape(8, 128).T for l in range(2)]).astype(np.float32))
    ident_bm = np.eye(128, dtype=ml_dtypes.bfloat16)
    ident_fm = np.eye(128, dtype=np.float32)
    trilv = np.ascontiguousarray(
        np.tril(np.ones((128, 128), dtype=np.float32)).T.astype(ml_dtypes.bfloat16))
    one16_a = np.ones((1, 16), dtype=ml_dtypes.bfloat16)
    crow_a = np.array([[1e-5, 1e-6, 1e-24, -0.5, -1.0, 1.0 / 1024, 0.0, 0.0]],
                      dtype=np.float32)

    in_maps = []
    cores = list(range(8)) if _cores is None else list(_cores)
    for core in cores:
        bi, h = core // 2, core % 2
        t0 = OWN * h
        xa = np.zeros((XN_N, D), dtype=np.float32)
        lo = t0 - 4096
        src_lo = max(lo, 0)
        xa[src_lo - lo: XN_N] = x[bi, src_lo: t0 + 4096]
        tabrows = np.arange(t0 + TAB0, t0 + TAB0 + TAB_N)
        tsel = np.clip(tabrows, 0, T - 1)
        zero = (tabrows < 0) | (tabrows >= T)
        def tab(a):
            v = a[tsel].copy()
            v[zero] = 0.0
            return np.ascontiguousarray(v)
        m = {
            "x_all": xa,
            "w_attn": np.ascontiguousarray(np.asarray(L0_attn_w_attn, dtype=np.float32)),
            "w_proj": np.ascontiguousarray(np.asarray(L0_attn_w_proj, dtype=np.float32)),
            "wqkv": np.ascontiguousarray(np.asarray(L0_ffn_wqkv, dtype=np.float32)),
            "wo": np.ascontiguousarray(np.asarray(L0_ffn_wo, dtype=np.float32)),
            "w1p": w1p_a, "w2p": w2p_a, "b1p": b1p_a,
            "lbv": lbv_a, "lgv": lgv_a, "b2v": b2v_a,
            "qsr": np.ascontiguousarray(np.asarray(L0_ffn_q_scale, dtype=np.float32).reshape(2, 1, 64)),
            "ksr": np.ascontiguousarray(np.asarray(L0_ffn_k_scale, dtype=np.float32).reshape(2, 1, 64)),
            "csq": tab(csq), "ssq": tab(ssq), "csk": tab(csk), "ssk": tab(ssk),
            "mks": _masks_for(t0),
            "ident_b": ident_bm, "ident_f": ident_fm, "trilT": trilv,
            "smask": np.full((1, 16), 1.0 if h == 1 else 0.0, dtype=ml_dtypes.bfloat16),
            "one16": one16_a, "crow": crow_a,
        }
        in_maps.append(m)

    key = (tuple(sorted(_dump)), _upto)
    if key not in _NC_CACHE:
        _NC_CACHE[key] = build_nc(dump=_dump, upto=_upto)
    nc = _NC_CACHE[key]
    res = run_bass_kernel_spmd(nc, in_maps, core_ids=list(range(len(cores))))
    if _dump:
        return res
    out = np.empty((B, T, D), dtype=np.float32)
    for i, core in enumerate(cores):
        bi, h = core // 2, core % 2
        out[bi, OWN * h: OWN * (h + 1)] = res.results[i]["out"]
    return out


# version 11
# speedup vs baseline: 1.3978x; 1.0238x over previous
import numpy as np
_LAST_SCHED = None
import ml_dtypes
import concourse.bass as bass
import concourse.mybir as mybir
from concourse.bass_utils import run_bass_kernel_spmd
from concourse.ap import AP

F32 = mybir.dt.float32
BF16 = mybir.dt.bfloat16
AF = mybir.ActivationFunctionType
ALU = mybir.AluOpType
AX = mybir.AxisListType

B, T, D = 4, 8192, 1024
OWN = 4096
WIN = 512
HID, HIDP = 2730, 2816
W1C = 2 * HIDP
NKC2 = 22
XN_N = 8192
NHIST = 4
FA0_Q0, FA0_NQ = -896, 39
FB0_Q0, FB0_NQ = -768, 38
FA1_Q0, FA1_NQ = -128, 33
FB1_Q0, FB1_NQ = 0, 32
X1_N = FA0_NQ * 128
X2_N = FB0_NQ * 128
X3_N = FA1_NQ * 128
TAB0 = -1408
TAB_N = 5504
NMASK = FA0_NQ + FA1_NQ

ENGINES = ("sync", "vector", "scalar", "tensor", "gpsimd")


class Sched:
    def __init__(self, nc):
        self.nc = nc
        self.q = {e: [] for e in ENGINES}
        self.marks = []
        self.sems = {}
        self.c = {}
        self.dma_chans = set()
        for n in ("pe", "ve", "ac", "gp", "bar"):
            self.sems[n] = nc.alloc_semaphore(n)
            self.c[n] = 0

    def chan(self, n):
        if n not in self.sems:
            self.sems[n] = self.nc.alloc_semaphore(n)
            self.c[n] = 0
            self.dma_chans.add(n)
        return n

    def emit(self, eng, fn, wait=(), inc=None, inc_k=1):
        w = tuple((s, v) for t_ in wait if t_ is not None for (s, v) in (t_,) if v > 0)
        tok = None
        if inc is not None:
            self.c[inc] += inc_k
            tok = (inc, self.c[inc])
        self.q[eng].append((w, fn, (self.sems[inc], inc_k) if inc else None,
                            eng in ("vector", "scalar", "gpsimd")))
        return tok

    def dma(self, eng, out, in_, wait=(), ch="wc"):
        sem = self.chan(ch)
        return self.emit(eng, lambda e, o=out, i=in_: e.dma_start(out=o, in_=i),
                         wait=wait, inc=sem, inc_k=16)

    def barrier(self):
        self.marks.append({e: len(self.q[e]) for e in ENGINES})
        self.c["bar"] += 5
        tgt = self.c["bar"]
        for eng in ENGINES:
            if eng == "sync":
                w = tuple((n, self.c[n]) for n in sorted(self.dma_chans)
                          if self.c[n] > 0)
            else:
                w = ()
            self.q[eng].append((w, lambda e: e.nop(), (self.sems["bar"], 1), False))
            self.q[eng].append(((("bar", tgt),), None, None, False))

    def realize(self):
        sems = self.sems
        with self.nc.Block() as block:
            def mk(engname):
                def body(e):
                    for w, fn, inc, drn in self.q[engname]:
                        for s, v in w:
                            e.wait_ge(sems[s], v)
                        if fn is None:
                            continue
                        ins = fn(e)
                        if inc is not None:
                            if ins is None:
                                if drn:
                                    e.drain()
                                ins = e.nop()
                            ins.then_inc(*inc)
                        if drn:
                            e.drain()
                return body
            block.sync(mk("sync"))
            block.vector(mk("vector"))
            block.scalar(mk("scalar"))
            block.tensor(mk("tensor"))
            block.gpsimd(mk("gpsimd"))


class Arena:
    def __init__(self, nc, base):
        self.nc = nc
        self.base = base
        self.cur = base
        self.n = 0

    def reset(self):
        self.cur = self.base

    def at(self, shape, dtype):
        nb = int(np.prod(shape[1:])) * mybir.dt.size(dtype)
        nb = (nb + 31) & ~31
        off = self.cur
        self.cur += nb
        self.n += 1
        assert self.cur <= getattr(self, "limit", 207 * 1024), \
            f"SBUF arena overflow: {self.cur - self.base}"
        return self.nc.alloc_sbuf_tensor_at(f"ar{self.n}", list(shape), dtype,
                                            offset=off)


def apm(base_ap, extra_off, dims):
    return AP(base_ap.tensor, base_ap.offset + extra_off, dims)


def build_nc(dump=(), upto=6):
    nc = bass.Bass("TRN2", target_bir_lowering=False, debug=False)

    def di(name, shape, dtype=F32):
        return nc.dram_tensor(name, list(shape), dtype, kind="ExternalInput")

    def dsc(name, shape, dtype):
        kind = "ExternalOutput" if name in dump else "Internal"
        return nc.dram_tensor(name, list(shape), dtype, kind=kind)

    x_all = di("x_all", [XN_N, D])
    w_attn = di("w_attn", [3072, D])
    w_proj = di("w_proj", [D, D])
    wqkv = di("wqkv", [2, D, 1536])
    wo = di("wo", [2, 512, D])
    w1p = di("w1p", [2, D, W1C])
    w2p = di("w2p", [2, HIDP, D])
    b1p = di("b1p", [2, 128, 44])
    lbv = di("lbv", [2, 128, 8], BF16)
    lgv = di("lgv", [2, 128, 8])
    b2v = di("b2v", [2, 128, 8])
    qsr = di("qsr", [2, 1, 64])
    ksr = di("ksr", [2, 1, 64])
    tabs = {n: di(n, [TAB_N, 64]) for n in ("csq", "ssq", "csk", "ssk")}
    mks = di("mks", [NMASK, 128, 640], BF16)
    ident_b = di("ident_b", [128, 128], BF16)
    ident_f = di("ident_f", [128, 128])
    trilT_i = di("trilT", [128, 128], BF16)
    smask = di("smask", [1, 16], BF16)
    one16 = di("one16", [1, 16], BF16)
    crow = di("crow", [1, 8])
    out_t = nc.dram_tensor("out", [OWN, D], F32, kind="ExternalOutput")

    xn_sc = dsc("xn_sc", [D, XN_N], BF16)
    ao_sc = dsc("ao_sc", [D, OWN], BF16)
    x1f_sc = dsc("x1f_sc", [D, X1_N], F32)
    x1b_sc = dsc("x1b_sc", [D, X1_N], BF16)
    x2f_sc = dsc("x2f_sc", [D, X2_N], F32)
    x2b_sc = dsc("x2b_sc", [D, X2_N], BF16)
    x3f_sc = dsc("x3f_sc", [D, X3_N], F32)
    x3b_sc = dsc("x3b_sc", [D, X3_N], BF16)
    rt_sc = dsc("rt_sc", [1, 2048], F32)
    dn_sc = dsc("dn_sc", [32, 128], BF16)
    ac_sc = dsc("ac_sc", [1, 1024], F32)

    def fm_read(t, ncols, c0, w, nch=8, ch0=0):
        return AP(t, ch0 * 128 * ncols + c0,
                  [[ncols, 128], [128 * ncols, nch], [1, w]])

    def fm_write(t, ncols, c0, w, nch=8, ch0=0):
        return AP(t, ch0 * 128 * ncols + c0,
                  [[ncols, 128], [128 * ncols, nch], [1, w]])

    ps_a = nc.alloc_psum_tensor("ps_a", [128, 512], F32)
    ps_b = nc.alloc_psum_tensor("ps_b", [128, 512], F32)
    ps_c = nc.alloc_psum_tensor("ps_c", [128, 512], F32)
    ps_d = nc.alloc_psum_tensor("ps_d", [128, 512], F32)
    ps_E2 = nc.alloc_psum_tensor("ps_E2", [128, 1024], F32)
    ps_G2 = nc.alloc_psum_tensor("ps_G2", [128, 1024], F32)

    P = nc.alloc_sbuf_tensor
    idb = P("idb", [128, 128], BF16)
    idf = P("idf", [128, 128], F32)
    trilT = P("trilT_s", [128, 128], BF16)
    cst = P("cst", [128, 8], F32)
    ones1 = P("ones1", [128, 1], BF16)
    ARENA_BYTES = 208896
    spacer = P("arena_spacer", [128, ARENA_BYTES], mybir.dt.uint8)
    arena_base = nc.lookup_mloc(spacer).addr

    S = Sched(nc)
    global _LAST_SCHED
    _LAST_SCHED = S
    AR = Arena(nc, arena_base)
    AR.limit = arena_base + ARENA_BYTES

    S.dma("sync", idb[:], ident_b.ap())
    S.dma("sync", idf[:], ident_f.ap())
    S.dma("sync", trilT[:], trilT_i.ap())
    S.dma("sync", cst[:], AP(crow, 0, [[0, 128], [1, 8]]))
    t_pre = S.dma("sync", ones1[:], AP(one16, 0, [[0, 128], [1, 1]]))

    assert upto >= 1
    xst = [AR.at([128, 1024], F32) for _ in range(4)]
    bn6 = AR.at([128, 12], F32)
    bn2 = AR.at([128, 2], F32)
    rstd = AR.at([128, 1], F32)
    nmr = AR.at([128, 1], F32)
    xnt = [AR.at([128, 1024], BF16) for _ in range(4)]
    xns = [AR.at([128, 1024], BF16) for _ in range(4)]
    done_x = [None] * 4
    done_tr = [None, None, None, None]
    done_st = [None, None, None, None]
    prev_t2 = None
    for j in range(64):
        sl = j % 4
        ld = S.dma("sync", xst[sl][:], AP(x_all, j * 128 * D, [[D, 128], [1, D]]),
                   wait=[done_x[sl], t_pre if j == 0 else None], ch=f"x{sl}")
        def f_bn(e, s=sl):
            e.bn_stats(out=bn6[:, 0:6], in_=xst[s][:, 0:512])
            e.drain()
            e.bn_stats(out=bn6[:, 6:12], in_=xst[s][:, 512:1024])
            e.drain()
            return e.bn_aggr(out=bn2[:], in_=bn6[:])
        t1 = S.emit("vector", f_bn, wait=[ld, prev_t2], inc="ve")
        t2a = S.emit("scalar", lambda e: e.activation(out=rstd[:], in_=bn2[:, 1:2],
                     func=AF.Sqrt, bias=cst[:, 0:1]), wait=[t1], inc="ac")
        t2 = S.emit("vector", lambda e: e.reciprocal(out=rstd[:], in_=rstd[:]),
                    wait=[t2a], inc="ve")
        prev_t2 = t2
        t3 = S.emit("vector", lambda e: e.tensor_scalar(out=nmr[:], in0=bn2[:, 0:1],
                    scalar1=rstd[:], scalar2=-1.0, op0=ALU.mult, op1=ALU.mult),
                    wait=[t2], inc="ve")
        pj = j % 4
        t4 = S.emit("vector", lambda e, s=sl, p=pj: e.tensor_scalar(
            out=xnt[p][:], in0=xst[s][:], scalar1=rstd[:], scalar2=nmr[:],
            op0=ALU.mult, op1=ALU.add), wait=[t3, done_tr[pj]], inc="ve")
        done_x[sl] = t4
        pv = (ps_G2[:].bitcast(BF16)[:, 0:1024], ps_G2[:].bitcast(BF16)[:, 1024:2048],
              ps_E2[:].bitcast(BF16)[:, 0:1024], ps_E2[:].bitcast(BF16)[:, 1024:2048])[pj]
        def f_tr(e, p=pj, pv=pv):
            for fc in range(8):
                ins_ = e.transpose(out=pv[:, fc * 128: (fc + 1) * 128],
                            in_=xnt[p][:, fc * 128: (fc + 1) * 128], identity=idb[:])
            return ins_
        t5 = S.emit("tensor", f_tr, wait=[t4, done_st[pj]], inc="pe")
        t6 = S.emit("scalar", lambda e, p=pj, pv=pv: e.copy(out=xns[p][:],
                    in_=pv[:, 0:1024]), wait=[t5], inc="ac")
        done_tr[pj] = t6
        done_st[pj] = S.dma("sync", fm_write(xn_sc, XN_N, j * 128, 128),
                            xns[pj][:].rearrange("p (c t) -> p c t", c=8), wait=[t6],
                            ch=f"st{pj}")
    S.barrier()

    if upto < 2:
        S.q["sync"].append((tuple((n, S.c[n]) for n in sorted(S.dma_chans)
                                  if S.c[n] > 0), None, None, False))
        S.realize()
        return nc
    AR.reset()
    wT = [AR.at([128, 3072], BF16) for _ in range(8)]
    wTp = [AR.at([128, 1024], BF16) for _ in range(8)]
    wst = [AR.at([128, 1024], F32) for _ in range(2)]
    wsb = AR.at([128, 1024], BF16)
    last_tr = t_pre
    prev_cast = None
    for widx, (wsrc, nrc, dst) in enumerate(((w_attn, 24, wT), (w_proj, 8, wTp))):
        for rc in range(nrc):
            sl = rc % 2
            ld = S.dma("sync", wst[sl][:], AP(wsrc, rc * 128 * D, [[D, 128], [1, D]]),
                       wait=[prev_cast])
            tc_ = S.emit("vector", lambda e, s=sl: e.tensor_copy(out=wsb[:], in_=wst[s][:]),
                         wait=[ld, last_tr], inc="ve")
            prev_cast = tc_
            pview = ps_G2[:].bitcast(BF16)[:, 0:1024]
            def f_tw(e):
                for ec in range(8):
                    ins_ = e.transpose(out=pview[:, ec * 128: (ec + 1) * 128],
                                in_=wsb[:, ec * 128: (ec + 1) * 128], identity=idb[:])
                return ins_
            t5 = S.emit("tensor", f_tw, wait=[tc_], inc="pe")
            def f_ev(e, rc=rc, dst=dst):
                for ec in range(8):
                    ins_ = e.copy(out=dst[ec][:, rc * 128: (rc + 1) * 128],
                           in_=pview[:, ec * 128: (ec + 1) * 128])
                return ins_
            last_tr = S.emit("scalar", f_ev, wait=[t5], inc="ac")

    xnt_a = [AR.at([128, 1024], BF16) for _ in range(2)]
    phik_tm1 = AR.at([128, 1024], BF16)
    v_ext1 = AR.at([128, 1040], BF16)
    tmpf = AR.at([128, 512], F32)
    tmpe = AR.at([128, 512], F32)
    SzA = ps_E2[0:128, 0:260]
    SzB = ps_E2[0:128, 512:772]

    def vext_ones(th, ntile=1):
        pitch = th[:].ap[0][0]
        if ntile == 1:
            return AP(th, 64, [[pitch, 128], [65, 16], [1, 1]])
        return AP(th, 64, [[pitch, 128], [1040, ntile], [65, 16], [1, 1]])

    smask_sb = AR.at([128, 16], BF16)
    t_vm0 = S.dma("sync", smask_sb[:], AP(smask, 0, [[0, 128], [1, 16]]),
                  wait=[last_tr])
    t_vm = S.emit("vector", lambda e: e.tensor_copy(
        out=vext_ones(v_ext1),
        in_=AP(smask_sb, 0, [[16, 128], [1, 16], [0, 1]])), wait=[t_vm0], inc="ve")

    def phi_chain(psrc, dstap, wait_t):
        a1 = S.emit("vector", lambda e: e.tensor_scalar_min(out=tmpf[:], in0=psrc,
                    scalar1=0.0), wait=wait_t, inc="ve")
        a2 = S.emit("scalar", lambda e: e.activation(out=tmpe[:], in_=tmpf[:],
                    func=AF.Exp), wait=[a1], inc="ac")
        a3 = S.emit("vector", lambda e, d=dstap: e.scalar_tensor_tensor(out=d,
                    in0=psrc, scalar=0.0, in1=tmpe[:], op0=ALU.max, op1=ALU.add),
                    wait=[a2], inc="ve")
        return a3

    ev_done = t_vm
    for j in range(32):
        ld = S.dma("sync", xnt_a[j % 2][:].rearrange("p (c t) -> p c t", c=8),
                   fm_read(xn_sc, XN_N, j * 128, 128), wait=[ev_done], ch="a1x")
        def f_kv(e, sl=j % 2):
            for fc in range(8):
                lhs = xnt_a[sl][:, fc * 128: (fc + 1) * 128]
                ins_ = e.matmul(out=ps_a[:], lhsT=lhs, rhs=wT[fc][:, 1024:1536], start=fc == 0, stop=fc == 7)
                ins_ = e.matmul(out=ps_b[:], lhsT=lhs, rhs=wT[fc][:, 1536:2048], start=fc == 0, stop=fc == 7)
                ins_ = e.matmul(out=ps_c[:], lhsT=lhs, rhs=wT[fc][:, 2048:2560], start=fc == 0, stop=fc == 7)
                ins_ = e.matmul(out=ps_d[:], lhsT=lhs, rhs=wT[fc][:, 2560:3072], start=fc == 0, stop=fc == 7)
            return ins_
        mm1 = S.emit("tensor", f_kv, wait=[ld, ev_done], inc="pe")
        p1 = phi_chain(ps_a[:], phik_tm1[:, 0:512], [mm1])
        p2 = phi_chain(ps_b[:], phik_tm1[:, 512:1024], [p1])
        pt1 = phik_tm1[:].ap[0][0]
        e1 = S.emit("scalar", lambda e: e.copy(
            out=AP(v_ext1, 0, [[1040, 128], [65, 8], [1, 64]]), in_=ps_c[:]),
            wait=[mm1], inc="ac")
        e2 = S.emit("scalar", lambda e: e.copy(
            out=AP(v_ext1, 8 * 65, [[1040, 128], [65, 8], [1, 64]]), in_=ps_d[:]),
            wait=[e1], inc="ac")
        def f_sz(e, j=j):
            for h in range(16):
                pj = h // 2
                dst = SzA if pj < 4 else SzB
                reg = dst[64 * (h % 2): 64 * (h % 2) + 64,
                          65 * (pj % 4): 65 * (pj % 4) + 65]
                ins_ = e.matmul(out=reg, lhsT=phik_tm1[:, 64 * h: 64 * h + 64],
                         rhs=v_ext1[:, 65 * h: 65 * h + 65],
                         start=(j == 0 and h % 8 < 2), stop=False,
                         skip_group_check=True)
            return ins_
        ev_done = S.emit("tensor", f_sz, wait=[p2, e2], inc="pe")

    xn_U = AR.at([128, 4096], BF16)
    phiq_fm = AR.at([128, 4096], BF16)
    phik_fm = AR.at([128, 4096], BF16)
    phik_tm4 = AR.at([128, 4096], BF16)
    v_ext4 = AR.at([128, 4160], BF16)
    Szb = AR.at([128, 520], BF16)
    sTp = [AR.at([128, 512], BF16) for _ in range(4)]
    y_fm = AR.at([128, 4096], BF16)
    rden_sb = AR.at([128, 2048], F32)
    rden_bc = AR.at([128, 2048], F32)
    ao_st = AR.at([128, 4096], BF16)

    t_vo2 = S.emit("vector", lambda e: e.memset(vext_ones(v_ext4, 4), 1.0),
                   wait=[ev_done], inc="ve")

    prev_scan_pe = ev_done
    prev_chunk_y = None
    prev_proj = t_vo2
    prev_d1 = None
    for u in range(8):
        ldU = S.dma("sync", xn_U[:].rearrange("p (c t) -> p c t", c=8),
                    fm_read(xn_sc, XN_N, 4096 + u * 512, 512), wait=[prev_proj],
                    ch="ldu")
        qk_done = []
        for oc in range(16):
            dstp = ps_G2[:, (oc % 2) * 512:(oc % 2) * 512 + 512]
            wprev = qk_done[-2] if len(qk_done) >= 2 else prev_proj
            def f_qk(e, oc=oc, dstp=dstp):
                for fc in range(8):
                    ins_ = e.matmul(out=dstp, lhsT=wT[fc][:, oc * 128: (oc + 1) * 128],
                             rhs=xn_U[:, fc * 512: (fc + 1) * 512],
                             start=fc == 0, stop=fc == 7)
                return ins_
            m = S.emit("tensor", f_qk, wait=[ldU, wprev], inc="pe")
            dst = (phiq_fm if oc < 8 else phik_fm)[:, (oc % 8) * 512:((oc % 8) + 1) * 512]
            qk_done.append(phi_chain(dstp, dst, [m]))
        va_done = []
        for tc in range(4):
            def f_v(e, tc=tc):
                for fc in range(8):
                    lhs = xn_U[:, fc * 512 + tc * 128: fc * 512 + tc * 128 + 128]
                    ins_ = e.matmul(out=ps_a[:], lhsT=lhs, rhs=wT[fc][:, 2048:2560],
                             start=fc == 0, stop=fc == 7)
                    ins_ = e.matmul(out=ps_b[:], lhsT=lhs, rhs=wT[fc][:, 2560:3072],
                             start=fc == 0, stop=fc == 7)
                return ins_
            m = S.emit("tensor", f_v, wait=[qk_done[-1] if tc == 0 else va_done[-1]],
                       inc="pe")
            c1 = S.emit("scalar", lambda e, tc=tc: e.copy(
                out=AP(v_ext4, tc * 1040, [[4160, 128], [65, 8], [1, 64]]),
                in_=ps_a[:]), wait=[m], inc="ac")
            c2 = S.emit("scalar", lambda e, tc=tc: e.copy(
                out=AP(v_ext4, tc * 1040 + 8 * 65, [[4160, 128], [65, 8], [1, 64]]),
                in_=ps_b[:]), wait=[c1], inc="ac")
            pview = ps_G2[:].bitcast(BF16)[:, 0:1024]
            def f_tk(e, tc=tc):
                for fc in range(8):
                    ins_ = e.transpose(out=pview[:, fc * 128: (fc + 1) * 128],
                                in_=phik_fm[:, fc * 512 + tc * 128: fc * 512 + tc * 128 + 128],
                                identity=idb[:])
                return ins_
            mt = S.emit("tensor", f_tk, wait=[c2, qk_done[-1]], inc="pe")
            ev = S.emit("scalar", lambda e, tc=tc: e.copy(
                out=phik_tm4[:, tc * 1024: (tc + 1) * 1024],
                in_=pview[:, 0:1024]), wait=[mt], inc="ac")
            va_done.append(ev)
        for cc in range(4):
            c = u * 4 + cc
            se1 = S.emit("vector", lambda e: e.tensor_copy(out=Szb[:, 0:260], in_=SzA),
                         wait=[prev_scan_pe, va_done[-1]], inc="ve")
            se2 = S.emit("vector", lambda e: e.tensor_copy(out=Szb[:, 260:520], in_=SzB),
                         wait=[], inc="ve")
            mask_toks = []
            nd = [ps_c[0:65, :], ps_d[0:65, :], ps_G2[0:65, 0:512], ps_G2[0:65, 512:1024]]
            pe_last = None
            nd_funcs = []
            for h in range(16):
                pj, par = h // 2, 64 * (h % 2)
                ping = (ps_a if h % 2 == 0 else ps_b)[0:128, 0:128]
                wait_st = [se2] if h == 0 else []
                if h >= 2:
                    wait_st.append(mask_toks[h - 2])
                if h < 2 and prev_chunk_y is not None:
                    wait_st.append(prev_chunk_y)
                def f_st(e, ping=ping, cc=cc, pj=pj, par=par):
                    return e.matmul(out=ping,
                             lhsT=phik_fm[par:par + 64, pj * 512 + cc * 128: pj * 512 + cc * 128 + 128],
                             rhs=phiq_fm[par:par + 64, pj * 512 + cc * 128: pj * 512 + cc * 128 + 128],
                             start=True, stop=True)
                tst = S.emit("tensor", f_st, wait=wait_st, inc="pe")
                tm = S.emit("vector", lambda e, h=h, ping=ping: e.tensor_tensor(
                    out=sTp[h // 4][:, 128 * (h % 4): 128 * (h % 4) + 128],
                    in0=ping, in1=trilT[:], op=ALU.mult), wait=[tst], inc="ve")
                mask_toks.append(tm)
                reg = nd[h // 4][0:65, 128 * (h % 4): 128 * (h % 4) + 128]
                def f_nd(e, h=h, cc=cc, pj=pj, par=par, reg=reg, c=c):
                    skp = (h % 2 == 1)
                    ins_ = e.matmul(out=reg, lhsT=v_ext4[:, cc * 1040 + 65 * h: cc * 1040 + 65 * h + 65],
                             rhs=sTp[h // 4][:, 128 * (h % 4): 128 * (h % 4) + 128],
                             start=(h % 4 < 2), stop=False, skip_group_check=skp)
                    ins_ = e.matmul(out=reg, lhsT=Szb[par:par + 64, 65 * pj: 65 * pj + 65],
                             rhs=phiq_fm[par:par + 64, pj * 512 + cc * 128: pj * 512 + cc * 128 + 128],
                             start=False, stop=(h % 4 >= 2), skip_group_check=skp)
                    dstz = SzA if pj < 4 else SzB
                    regz = dstz[par:par + 64, 65 * (pj % 4): 65 * (pj % 4) + 65]
                    ins_ = e.matmul(out=regz, lhsT=phik_tm4[:, cc * 1024 + 64 * h: cc * 1024 + 64 * h + 64],
                             rhs=v_ext4[:, cc * 1040 + 65 * h: cc * 1040 + 65 * h + 65],
                             start=False, stop=False, skip_group_check=True)
                    return ins_
                nd_funcs.append((f_nd, tm))
                if h >= 1:
                    fn_, tm_ = nd_funcs[h - 1]
                    pe_last = S.emit("tensor", fn_, wait=[tm_], inc="pe")
            fn_, tm_ = nd_funcs[15]
            pe_last = S.emit("tensor", fn_, wait=[tm_], inc="pe")
            prev_scan_pe = pe_last
            rc1 = None
            for bk in range(4):
                src = nd[bk][64:65, 0:512]
                rc0 = S.emit("scalar", lambda e, bk=bk, src=src: e.activation(
                    out=rden_sb[64:65, 512 * bk: 512 * bk + 512], in_=src,
                    func=AF.Ln, bias=cst[64:65, 1:2]),
                    wait=[pe_last if bk == 0 else None, prev_d1 if bk == 0 else None],
                    inc="ac")
                rc1 = S.emit("scalar", lambda e, bk=bk: e.activation(
                    out=rden_sb[64:65, 512 * bk: 512 * bk + 512],
                    in_=rden_sb[64:65, 512 * bk: 512 * bk + 512],
                    func=AF.Exp, scale=cst[64:65, 4:5]), wait=[rc0], inc="ac")
            d1 = S.dma("sync", rt_sc.ap(), rden_sb[64:65, :], wait=[rc1], ch="rt1")
            prev_d1 = d1
            d2 = S.dma("sync", rden_bc[:], AP(rt_sc, 0, [[0, 128], [1, 2048]]),
                       wait=[d1, prev_chunk_y], ch="rt2")
            ylast = None
            for bk in range(4):
                ndap = nd[bk]
                pitch = ndap.ap[0][0]
                in0e = apm(ndap, 0, [[pitch, 64], [256, 2], [1, 128]])
                in0o = apm(ndap, 128, [[pitch, 64], [256, 2], [1, 128]])
                in1e = AP(rden_bc, 512 * bk, [[2048, 64], [256, 2], [1, 128]])
                in1o = AP(rden_bc, 64 * 2048 + 512 * bk + 128, [[2048, 64], [256, 2], [1, 128]])
                oute = AP(y_fm, 512 * (2 * bk) + 128 * cc, [[4096, 64], [512, 2], [1, 128]])
                outo = AP(y_fm, 64 * 4096 + 512 * (2 * bk) + 128 * cc, [[4096, 64], [512, 2], [1, 128]])
                S.emit("vector", lambda e, a=oute, b=in0e, c_=in1e: e.tensor_tensor(
                    out=a, in0=b, in1=c_, op=ALU.mult), wait=[d2, pe_last], inc="ve")
                ylast = S.emit("vector", lambda e, a=outo, b=in0o, c_=in1o: e.tensor_tensor(
                    out=a, in0=b, in1=c_, op=ALU.mult), wait=[], inc="ve")
            prev_chunk_y = ylast
        evs = None
        for fc in range(8):
            ping = ps_a if fc % 2 == 0 else ps_b
            def f_pr(e, fc=fc, ping=ping):
                for kc in range(8):
                    ins_ = e.matmul(out=ping[:], lhsT=wTp[kc][:, fc * 128: (fc + 1) * 128],
                             rhs=y_fm[:, kc * 512: (kc + 1) * 512],
                             start=kc == 0, stop=kc == 7)
                return ins_
            m = S.emit("tensor", f_pr, wait=[prev_chunk_y,
                       evs if fc >= 2 else None], inc="pe")
            evs = S.emit("scalar", lambda e, fc=fc, ping=ping: e.copy(
                out=ao_st[:, fc * 512: (fc + 1) * 512], in_=ping[:]),
                wait=[m, prev_proj if fc == 0 else None], inc="ac")
        prev_proj = S.dma("sync", fm_write(ao_sc, OWN, u * 512, 512),
                          ao_st[:].rearrange("p (c t) -> p c t", c=8), wait=[evs],
                          ch="aos")
    S.barrier()

    def stage_fa(l, in_b16, in_n, in_c0, res_scr, res_n, res_f32, res_c0,
                 out_f32, out_b16, out_n, nq, tab_off, mk_base):
        AR.reset()
        wqk = [AR.at([128, 1536], BF16) for _ in range(8)]
        wos = [AR.at([128, 1024], BF16) for _ in range(4)]
        wstg = [AR.at([128, 1536], F32) for _ in range(2)]
        xst_g = [AR.at([128, 4096], BF16) for _ in range(2)]
        kr_buf = [AR.at([128, 1024], BF16) for _ in range(4)]
        v_sl = [AR.at([128, 512], BF16) for _ in range(8)]
        q4 = AR.at([128, 2048], BF16)
        k4 = AR.at([128, 2048], BF16)
        sq4 = AR.at([128, 2048], BF16)
        sq4k = AR.at([128, 2048], BF16)
        ss32 = AR.at([128, 64], F32)
        ri32 = AR.at([128, 64], F32)
        rq4 = AR.at([128, 2048], BF16)
        qhs4 = AR.at([128, 2048], BF16)
        t14 = AR.at([128, 2048], BF16)
        t24 = AR.at([128, 2048], BF16)
        qr4 = AR.at([128, 2048], BF16)
        kr4 = AR.at([128, 2048], BF16)
        qr_fm4 = AR.at([128, 2048], BF16)
        qsd = AR.at([128, 64], F32)
        ksd = AR.at([128, 64], F32)
        tab4 = {n: [AR.at([128, 256], F32) for _ in range(2)] for n in tabs}
        mkt = [AR.at([128, 2560], BF16) for _ in range(2)]
        nmax = [AR.at([128, 4], F32) for _ in range(2)]
        pbf = [AR.at([128, 640], BF16) for _ in range(2)]
        den = [AR.at([128, 2], F32) for _ in range(2)]
        rdn = [AR.at([128, 1], F32) for _ in range(2)]
        pT_sb = [AR.at([128, 640], BF16) for _ in range(2)]
        den_g = AR.at([128, 32], F32)
        rdn_g = AR.at([128, 32], F32)
        rdn_t = AR.at([32, 128], BF16)
        rden_sb = [AR.at([128, 512], BF16) for _ in range(4)]
        o_sb = [AR.at([128, 512], BF16) for _ in range(4)]
        res_tf = AR.at([128, 4096], F32 if res_f32 else BF16)
        x1o = AR.at([128, 4096], F32)
        x1ob = AR.at([128, 4096], BF16)

        tprev = None
        for fc in range(8):
            ld = S.dma("sync", wstg[fc % 2][:], AP(wqkv, l * D * 1536 + fc * 128 * 1536,
                       [[1536, 128], [1, 1536]]), wait=[tprev])
            tprev = S.emit("vector", lambda e, fc=fc, s=fc % 2: e.tensor_copy(
                out=wqk[fc][:], in_=wstg[s][:]), wait=[ld], inc="ve")
        for kc in range(4):
            ld = S.dma("sync", wstg[kc % 2][:, 0:1024], AP(wo, l * 512 * D + kc * 128 * D,
                       [[D, 128], [1, D]]), wait=[tprev])
            tprev = S.emit("vector", lambda e, kc=kc, s=kc % 2: e.tensor_copy(
                out=wos[kc][:], in_=wstg[s][:, 0:1024]), wait=[ld], inc="ve")
        S.dma("sync", qsd[:], AP(qsr, l * 64, [[0, 128], [1, 64]]), ch="misc")
        t_ks = S.dma("sync", ksd[:], AP(ksr, l * 64, [[0, 128], [1, 64]]), ch="misc")

        ntile = NHIST + nq
        prev_a3 = (None, None)
        prev_mpt = (None, None)
        prev_dr = (None, None)
        prev_tin = None
        prev_wo = tprev
        prev_x1 = t_ks
        prev_x1f = None
        prev_heads_done = None

        def emit_loads(g0x, wts):
            gw_ = min(4, ntile - g0x)
            gt_ = gw_ * 128
            gci_ = (g0x // 4) % 2
            chs = f"fa{l}_{gci_}"
            cin = in_c0 + g0x * 128
            S.dma("sync", AP(xst_g[gci_], 0, [[4096, 128], [512, 4], [1, gt_]]),
                  fm_read(in_b16, in_n, cin, gt_, nch=4, ch0=0),
                  wait=wts, ch=chs)
            tl = S.dma("sync", AP(xst_g[gci_], 4 * 512, [[4096, 128], [512, 4], [1, gt_]]),
                       fm_read(in_b16, in_n, cin - 1, gt_, nch=4, ch0=4), ch=chs)
            for n in tabs:
                tl = S.dma("sync", AP(tab4[n][gci_], 0, [[256, 128], [64, gw_], [1, 64]]),
                           AP(tabs[n], (tab_off + g0x * 128) * 64,
                              [[64, 128], [128 * 64, gw_], [1, 64]]), ch=chs)
            if g0x > 0:
                tl = S.dma("sync", AP(mkt[gci_], 0, [[2560, 128], [640, gw_], [1, 640]]),
                           AP(mks, (mk_base + g0x - NHIST) * 128 * 640,
                              [[640, 128], [128 * 640, gw_], [1, 640]]), ch=chs)
            return tl

        pend_ld = {0: emit_loads(0, [prev_wo, prev_x1])}
        pg = {"qkv": None, "ck": None, "rc": None}
        g0 = 0
        while g0 < ntile:
            gtiles = list(range(g0, min(g0 + 4, ntile)))
            gw = len(gtiles)
            gt = gw * 128
            gci = (g0 // 4) % 2
            is_hist = (g0 == 0)
            tile_ld = pend_ld.pop(g0)
            ev_q, ev_k, ev_v = {}, {}, {}
            for ti, st_ in enumerate(gtiles):
                psq = (ps_a if ti % 2 == 0 else ps_b)[:]
                psk = (ps_c if ti % 2 == 0 else ps_d)[:]
                psv = ps_E2[:, (ti % 2) * 512:(ti % 2) * 512 + 512]
                wq_ = [tile_ld if ti == 0 else None,
                       prev_x1 if ti == 0 else None]
                if ti == 0:
                    wq_ += [prev_a3[0], prev_a3[1], prev_heads_done]
                if ti >= 2:
                    wq_.append(ev_v[ti - 2])
                def f_qkv(e, gci=gci, ti=ti, psq=psq, psk=psk, psv=psv):
                    for fc in range(8):
                        lhs = xst_g[gci][:, fc * 512 + ti * 128: fc * 512 + ti * 128 + 128]
                        ins_ = e.matmul(out=psq, lhsT=lhs, rhs=wqk[fc][:, 0:512],
                                 start=fc == 0, stop=fc == 7)
                        ins_ = e.matmul(out=psk, lhsT=lhs, rhs=wqk[fc][:, 512:1024],
                                 start=fc == 0, stop=fc == 7)
                        ins_ = e.matmul(out=psv, lhsT=lhs, rhs=wqk[fc][:, 1024:1536],
                                 start=fc == 0, stop=fc == 7)
                    return ins_
                m = S.emit("tensor", f_qkv, wait=wq_, inc="pe")
                ev_q[ti] = S.emit("scalar", lambda e, ti=ti, psq=psq: e.copy(
                    out=q4[:, ti * 512: ti * 512 + 512], in_=psq), wait=[m], inc="ac")
                ev_k[ti] = S.emit("scalar", lambda e, ti=ti, psk=psk: e.copy(
                    out=k4[:, ti * 512: ti * 512 + 512], in_=psk), wait=[], inc="ac")
                ev_v[ti] = S.emit("scalar", lambda e, s=st_, psv=psv: e.copy(
                    out=v_sl[s % 8][:], in_=psv), wait=[], inc="ac")
            def chain_pre(src4, sqb, cols0, ev_last, gt=gt, gw=gw):
                c2 = S.emit("scalar", lambda e: e.activation(out=sqb[:, 0:gt * 4],
                            in_=src4[:, 0:gt * 4], func=AF.Square),
                            wait=[ev_last], inc="ac")
                c3 = S.emit("vector", lambda e: e.reduce_sum(
                    out=ss32[:, cols0:cols0 + gw * 8],
                    in_=AP(sqb, 0, [[2048, 128], [64, gw * 8], [1, 64]]),
                    axis=AX.X), wait=[c2], inc="ve")
                return c3
            def chain_main(src4, dst4, sd, cs, sn, cols0, c3, gt=gt, gw=gw, gci=gci):
                c4a = S.emit("scalar", lambda e: e.activation(
                    out=ri32[:, cols0:cols0 + gw * 8], in_=ss32[:, cols0:cols0 + gw * 8],
                    func=AF.Ln, bias=cst[:, 2:3]), wait=[c3], inc="ac")
                c4 = S.emit("scalar", lambda e: e.activation(
                    out=ri32[:, cols0:cols0 + gw * 8], in_=ri32[:, cols0:cols0 + gw * 8],
                    func=AF.Exp, scale=cst[:, 3:4]), wait=[c4a], inc="ac")
                c5 = S.emit("vector", lambda e, sd=sd: e.tensor_tensor(
                    out=AP(rq4, 0, [[2048, 128], [512, gw], [64, 8], [1, 64]]),
                    in0=AP(ri32, cols0, [[64, 128], [8, gw], [1, 8], [0, 64]]),
                    in1=AP(sd, 0, [[64, 128], [0, gw], [0, 8], [1, 64]]),
                    op=ALU.mult), wait=[c4], inc="ve")
                c6 = S.emit("vector", lambda e, src4=src4: e.tensor_tensor(
                    out=qhs4[:, 0:gt * 4], in0=src4[:, 0:gt * 4], in1=rq4[:, 0:gt * 4],
                    op=ALU.mult), wait=[c5], inc="ve")
                c7 = S.emit("vector", lambda e, cs=cs: e.tensor_tensor(
                    out=AP(t14, 0, [[2048, 128], [512, gw], [64, 8], [1, 64]]),
                    in0=AP(qhs4, 0, [[2048, 128], [512, gw], [64, 8], [1, 64]]),
                    in1=AP(tab4[cs][gci], 0, [[256, 128], [64, gw], [0, 8], [1, 64]]),
                    op=ALU.mult), wait=[c6], inc="ve")
                c8 = S.emit("vector", lambda e, sn=sn: e.tensor_tensor(
                    out=AP(t24, 0, [[2048, 128], [512, gw], [64, 8], [32, 2], [1, 32]]),
                    in0=AP(qhs4, 32, [[2048, 128], [512, gw], [64, 8], [-32, 2], [1, 32]]),
                    in1=AP(tab4[sn][gci], 0, [[256, 128], [64, gw], [0, 8], [32, 2], [1, 32]]),
                    op=ALU.mult), wait=[c7], inc="ve")
                c9 = S.emit("vector", lambda e, dst4=dst4: e.tensor_tensor(
                    out=dst4[:, 0:gt * 4], in0=t14[:, 0:gt * 4], in1=t24[:, 0:gt * 4],
                    op=ALU.add), wait=[c8], inc="ve")
                return c9
            qc3 = chain_pre(q4, sq4, 0, ev_q[gw - 1])
            kc3 = chain_pre(k4, sq4k, 32, ev_k[gw - 1])
            cq = chain_main(q4, qr4, qsd, "csq", "ssq", 0, qc3)
            ck = chain_main(k4, kr4, ksd, "csk", "ssk", 32, kc3)
            if g0 + 4 < ntile:
                pend_ld[g0 + 4] = emit_loads(
                    g0 + 4, [pg["qkv"], pg["ck"], pg["rc"]])
            pg = {"qkv": m, "ck": ck, "rc": None}
            ek_tok = {}
            pviews = (ps_G2[:].bitcast(BF16)[:, 0:1024],
                      ps_G2[:].bitcast(BF16)[:, 1024:2048])
            for ti, st_ in enumerate(gtiles):
                pview = pviews[ti % 2]
                def f_tq(e, ti=ti, pview=pview):
                    for jj in range(4):
                        ins_ = e.transpose(out=pview[:, jj * 128: (jj + 1) * 128],
                                    in_=qr4[:, ti * 512 + jj * 128: ti * 512 + (jj + 1) * 128],
                                    identity=idb[:])
                        ins_ = e.transpose(out=pview[:, 512 + jj * 128: 512 + (jj + 1) * 128],
                                    in_=kr4[:, ti * 512 + jj * 128: ti * 512 + (jj + 1) * 128],
                                    identity=idb[:])
                    return ins_
                mt = S.emit("tensor", f_tq, wait=[cq, ck] if ti == 0 else
                            [ek_tok[ti - 2]] if ti >= 2 else [], inc="pe")
                e1 = S.emit("scalar", lambda e, ti=ti, pview=pview: e.copy(
                    out=qr_fm4[:, ti * 512: (ti + 1) * 512], in_=pview[:, 0:512]),
                    wait=[mt], inc="ac")
                kcol = 512 + (st_ % 4) * 128
                def f_ek(e, kcol=kcol, pview=pview):
                    for p_ in range(4):
                        ins_ = e.copy(out=kr_buf[p_][:, kcol: kcol + 128],
                               in_=pview[:, 512 + p_ * 128: 512 + (p_ + 1) * 128])
                    return ins_
                ek_tok[ti] = S.emit("scalar", f_ek, wait=[e1], inc="ac")
            front_done = ek_tok[gw - 1]
            if is_hist:
                def f_slide0(e):
                    for p_ in range(4):
                        ins_ = e.tensor_copy(out=kr_buf[p_][:, 0:512],
                                             in_=kr_buf[p_][:, 512:1024])
                    return ins_
                prev_wo = S.emit("vector", f_slide0, wait=[front_done], inc="ve")
                g0 += 4
                continue
            o_done = []
            for ti, st_ in enumerate(gtiles):
                cc = ti
                avlast = None
                a3tok = [None] * 8
                a5tok = [None] * 8
                drtok = [None] * 8
                mpt_tok = [None] * 8

                def emit_pt(hh):
                    ptp = (ps_a if hh % 2 == 0 else ps_b)[:].bitcast(BF16)
                    def f_pt(e, ptp=ptp, ii=hh % 2):
                        for jj in range(5):
                            ins_ = e.transpose(out=ptp[:, jj * 128: (jj + 1) * 128],
                                        in_=pbf[ii][:, jj * 128: (jj + 1) * 128],
                                        identity=idb[:])
                        return ins_
                    mpt = S.emit("tensor", f_pt, wait=[a5tok[hh]], inc="pe")
                    mpt_tok[hh] = mpt
                    a6 = S.emit("scalar", lambda e, ptp=ptp, ii=hh % 2: e.copy(
                                out=pT_sb[ii][:], in_=ptp[:, 0:640]),
                                wait=[mpt], inc="ac")
                    return a6

                def emit_av(hh, a6, ti=ti, st_=st_):
                    def f_av(e, hh=hh, s=st_):
                        for jj in range(5):
                            vsl = v_sl[(s - 4 + jj) % 8]
                            ins_ = e.matmul(out=ps_c[64 * (hh % 2):64 * (hh % 2) + 64,
                                             128 * (hh // 2): 128 * (hh // 2) + 128],
                                     lhsT=vsl[:, 64 * hh: 64 * hh + 64],
                                     rhs=pT_sb[hh % 2][:, jj * 128: (jj + 1) * 128],
                                     start=(hh <= 1 and jj == 0),
                                     stop=(hh >= 6 and jj == 4),
                                     skip_group_check=(hh % 2 == 1))
                        return ins_
                    return S.emit("tensor", f_av, wait=[a6, ev_v[ti]], inc="pe")

                a6_pend = None
                for hh in range(8):
                    pj, par, ii = hh // 2, 64 * (hh % 2), hh % 2
                    if hh >= 2:
                        a6_pend = emit_pt(hh - 2)
                    sim2 = ps_E2 if ii == 0 else ps_G2
                    wsim = []
                    if hh <= 1 and ti == 0:
                        wsim.append(front_done)
                    if hh >= 2:
                        wsim.append(a3tok[hh - 2])
                    elif prev_a3[ii] is not None:
                        wsim.append(prev_a3[ii])
                    def f_sim(e, pj=pj, par=par, sim2=sim2, cc=cc, ti=ti):
                        lhs = qr_fm4[par:par + 64, ti * 512 + pj * 128: ti * 512 + (pj + 1) * 128]
                        ins_ = e.matmul(out=sim2[:, 0:512], lhsT=lhs,
                                 rhs=kr_buf[pj][par:par + 64, cc * 128: cc * 128 + 512],
                                 start=True, stop=True)
                        ins_ = e.matmul(out=sim2[:, 512:640], lhsT=lhs,
                                 rhs=kr_buf[pj][par:par + 64, cc * 128 + 512: cc * 128 + 640],
                                 start=True, stop=True)
                        return ins_
                    msim = S.emit("tensor", f_sim, wait=wsim, inc="pe")
                    x1_ = S.emit("scalar", lambda e, sim2=sim2, ii=ii: e.activation(
                        out=pbf[ii][:, 0:640], in_=sim2[:, 0:640], func=AF.Exp),
                        wait=[msim, mpt_tok[hh - 2] if hh >= 2 else None,
                              drtok[hh - 2] if hh >= 2 else None,
                              prev_mpt[ii] if hh < 2 else None,
                              prev_dr[ii] if hh < 2 else None], inc="ac")
                    a3tok[hh] = x1_
                    m1 = S.emit("vector", lambda e, ii=ii, gci=gci, ti=ti:
                        e.tensor_tensor(out=pbf[ii][:], in0=pbf[ii][:],
                                        in1=mkt[gci][:, ti * 640: ti * 640 + 640],
                                        op=ALU.mult), wait=[x1_], inc="ve")
                    drtok[hh] = S.emit("vector", lambda e, ii=ii, col=ti * 8 + hh:
                        e.reduce_sum(out=den_g[:, col:col + 1],
                                     in_=pbf[ii][:, 0:640], axis=AX.X),
                        wait=[m1], inc="ve")
                    a5tok[hh] = m1
                    if hh >= 2:
                        avlast = emit_av(hh - 2, a6_pend)
                a6p = emit_pt(6)
                a7p = emit_pt(7)
                avlast = emit_av(6, a6p)
                avlast = emit_av(7, a7p)
                prev_a3 = (a3tok[6], a3tok[7])
                prev_mpt = (mpt_tok[6], mpt_tok[7])
                prev_dr = (drtok[6], drtok[7])
                def f_eo(e, cc=cc):
                    for p_ in range(4):
                        ins_ = e.copy(out=o_sb[p_][:, cc * 128: (cc + 1) * 128],
                               in_=ps_c[:, 128 * p_: 128 * (p_ + 1)])
                    return ins_
                prev_wo = S.emit("scalar", f_eo, wait=[avlast], inc="ac")
                o_done.append(prev_wo)
            prev_heads_done = prev_wo
            rc = S.emit("vector", lambda e, gw=gw: e.reciprocal(
                out=rdn_g[:, 0:gw * 8], in_=den_g[:, 0:gw * 8]), wait=[], inc="ve")
            pg["rc"] = rc
            mtd = S.emit("tensor", lambda e, gw=gw: e.transpose(
                out=ps_d[0:gw * 8, 0:128],
                in_=rdn_g[:, 0:gw * 8], identity=idf[:]), wait=[rc], inc="pe")
            aev = S.emit("scalar", lambda e, gw=gw: e.copy(
                out=rdn_t[0:gw * 8, :],
                in_=ps_d[0:gw * 8, 0:128]), wait=[mtd], inc="ac")
            dch = f"dn{l}"
            o1 = S.dma("sync", AP(dn_sc, 0, [[128, gw * 8], [1, 128]]),
                       rdn_t[0:gw * 8, :], wait=[aev, prev_tin], ch=dch)
            tin = None
            for p_ in range(4):
                for half in range(2):
                    hq = 2 * p_ + half
                    base = rden_sb[p_][64 * half:64 * half + 64, 0:512]
                    tin = S.dma("sync", apm(base, 0, [[512, 64], [128, gw], [1, 128]]),
                                AP(dn_sc, hq * 128, [[0, 64], [8 * 128, gw], [1, 128]]),
                                wait=[o1], ch=dch)
            prev_tin = tin
            ttn = None
            for p_ in range(4):
                ttn = S.emit("vector", lambda e, p_=p_, nt2=gw * 128: e.tensor_tensor(
                    out=o_sb[p_][:, 0:nt2], in0=o_sb[p_][:, 0:nt2],
                    in1=rden_sb[p_][:, 0:nt2], op=ALU.mult),
                    wait=[tin if p_ == 0 else None,
                          o_done[-1] if p_ == 0 else None], inc="ve")
            n_t = gt
            lres = S.dma("sync", AP(res_tf, 0, [[4096, 128], [n_t, 8], [1, n_t]]),
                         fm_read(res_scr, res_n, res_c0 + (g0 - NHIST) * 128, n_t),
                         wait=[o_done[-1]], ch="lres")
            x1done = []
            for fc in range(8):
                ping = ps_E2[:, (fc % 2) * 512:(fc % 2) * 512 + 512]
                def f_wo(e, fc=fc, ping=ping, n_t=n_t):
                    for p_ in range(4):
                        ins_ = e.matmul(out=ping[:, 0:n_t],
                                 lhsT=wos[p_][:, fc * 128: (fc + 1) * 128],
                                 rhs=o_sb[p_][:, 0:n_t],
                                 start=p_ == 0, stop=p_ == 3)
                    return ins_
                m = S.emit("tensor", f_wo, wait=[ttn, lres,
                           x1done[-2] if len(x1done) >= 2 else None], inc="pe")
                x1done.append(S.emit("vector", lambda e, fc=fc, ping=ping, n_t=n_t:
                    e.tensor_add(out=AP(x1o, fc * 512, [[4096, 128], [1, n_t]]),
                                 in0=ping[:, 0:n_t],
                                 in1=AP(res_tf, fc * n_t, [[4096, 128], [1, n_t]])),
                    wait=[m], inc="ve"))
            cb = S.emit("gpsimd", lambda e, n_t=n_t: e.tensor_copy(
                out=AP(x1ob, 0, [[4096, 128], [512, 8], [1, n_t]]),
                in_=AP(x1o, 0, [[4096, 128], [512, 8], [1, n_t]])),
                wait=x1done, inc="gp")
            oc0 = (g0 - NHIST) * 128
            s1 = S.dma("sync", fm_write(out_f32, out_n, oc0, n_t),
                       AP(x1o, 0, [[4096, 128], [512, 8], [1, n_t]]), wait=x1done,
                       ch="fs1")
            s2 = S.dma("sync", fm_write(out_b16, out_n, oc0, n_t),
                       AP(x1ob, 0, [[4096, 128], [512, 8], [1, n_t]]), wait=[cb],
                       ch="fs2")
            prev_x1 = s2
            prev_x1f = s1
            if g0 + 4 < ntile:
                def f_slide(e):
                    for p_ in range(4):
                        ins_ = e.tensor_copy(out=kr_buf[p_][:, 0:512],
                                             in_=kr_buf[p_][:, 512:1024])
                    return ins_
                prev_wo = S.emit("vector", f_slide,
                                 wait=[prev_wo, prev_x1, prev_x1f], inc="ve")
            g0 += 4
        S.barrier()

    def stage_fb(l, in_f32, in_b16, in_n, in_c0, nq, out_f32, out_b16, out_n,
                 assemble=False):
        AR.reset()
        w1s = [AR.at([128, W1C], BF16) for _ in range(8)]
        w2s = [AR.at([128, 1024], BF16) for _ in range(NKC2)]
        wst_off = AR.cur
        wstg = AR.at([128, W1C], F32)
        wraw_off = AR.cur
        wraw = AR.at([128, W1C], BF16)
        sq2 = AR.at([128, 512], BF16)
        strow = AR.at([128, 1024], F32)
        glu0_off = AR.cur
        glu_sb = [AR.at([128, 512], BF16) for _ in range(NKC2)]
        lgc = AR.at([128, 8], F32)
        lbc = AR.at([128, 8], BF16)
        b2c = AR.at([128, 8], F32)
        b1es = AR.at([128, 44], F32)
        b1pp = AR.at([128, 44], F32)
        acbc = AR.at([128, 1024], F32)
        res2 = [AR.at([128, 512], F32) for _ in range(2)]
        gact2 = [AR.at([128, 512], BF16) for _ in range(2)]
        xs1 = nc.alloc_sbuf_tensor_at(f"xs1_{l}", [128, 4096], BF16, offset=wst_off)
        h_sb = nc.alloc_sbuf_tensor_at(f"hsb_{l}", [128, 4096], BF16,
                                       offset=wst_off + 8192)
        x2st = nc.alloc_sbuf_tensor_at(f"x2st{l}", [128, 4096], F32, offset=wraw_off)
        if assemble:
            att_t = [nc.alloc_sbuf_tensor_at(f"att{l}_{i}", [128, 512], BF16,
                     offset=wst_off + 16384 + i * 1024) for i in range(2)]
            xab = [nc.alloc_sbuf_tensor_at(f"xab{l}_{i}", [128, 1024], F32,
                   offset=wst_off + 8192 + i * 4096) for i in range(2)]

        S.dma("sync", lgc[:], AP(lgv, l * 128 * 8, [[8, 128], [1, 8]]), ch="misc")
        S.dma("sync", lbc[:], AP(lbv, l * 128 * 8, [[8, 128], [1, 8]]), ch="misc")
        S.dma("sync", b2c[:], AP(b2v, l * 128 * 8, [[8, 128], [1, 8]]), ch="misc")
        t0 = S.dma("sync", b1pp[:], AP(b1p, l * 128 * 44, [[44, 128], [1, 44]]),
                   ch="misc")
        tprev = None
        pe_b1 = None
        for kc in range(8):
            ld = S.dma("sync", wstg[:], AP(w1p, l * D * W1C + kc * 128 * W1C,
                       [[W1C, 128], [1, W1C]]),
                       wait=[tprev, t0 if kc == 0 else None])
            c1 = S.emit("vector", lambda e: e.tensor_copy(out=wraw[:],
                        in_=wstg[:]), wait=[ld, pe_b1], inc="ve")
            def f_b1(e, kc=kc):
                for oc in range(44):
                    ins_ = e.matmul(out=ps_c[0:128, oc:oc + 1],
                             lhsT=wraw[:, oc * 128: (oc + 1) * 128],
                             rhs=lbc[:, kc:kc + 1],
                             start=(kc == 0 and oc == 0),
                             stop=(kc == 7 and oc == 43))
                return ins_
            pe_b1 = S.emit("tensor", f_b1, wait=[c1], inc="pe")
            tprev = S.emit("vector", lambda e, kc=kc: e.tensor_scalar_mul(
                out=w1s[kc][:], in0=wstg[:], scalar1=lgc[:, kc:kc + 1]),
                wait=[c1], inc="ve")
        tb1 = S.emit("vector", lambda e: e.tensor_add(out=b1es[:], in0=ps_c[:, 0:44],
                     in1=b1pp[:]), wait=[pe_b1, tprev], inc="ve")
        w2cast = []
        for kc in range(NKC2):
            half = (kc % 2) * 1024
            ld = S.dma("sync", wstg[:, half: half + 1024],
                       AP(w2p, l * HIDP * D + kc * 128 * D, [[D, 128], [1, D]]),
                       wait=[w2cast[kc - 2] if kc >= 2 else tprev], ch=f"wc{kc % 2}")
            w2cast.append(S.emit("vector", lambda e, kc=kc, half=half: e.tensor_copy(
                out=w2s[kc][:], in_=wstg[:, half: half + 1024]), wait=[ld], inc="ve"))
        tprev = w2cast[-1]

        prev_st = tb1
        prev_grp = tprev
        prev_la = tb1
        prev_s1 = None
        g0 = 0
        while g0 < nq:
            gw = min(4, nq - g0)
            n_t = gw * 128
            cin = in_c0 + g0 * 128
            l1 = S.dma("sync", AP(xs1, 0, [[4096, 128], [512, 4], [1, n_t]]),
                       fm_read(in_b16, in_n, cin, n_t, nch=4, ch0=0),
                       wait=[prev_grp, prev_s1], ch="fbin")
            l2 = S.dma("sync", AP(xs1, 4 * 512, [[4096, 128], [512, 4], [1, n_t]]),
                       fm_read(in_b16, in_n, cin - 1, n_t, nch=4, ch0=4), ch="fbin")
            pe_st = None
            sq_prev = None
            for fc in range(8):
                sq_prev = S.emit("scalar", lambda e, fc=fc, n_t=n_t: e.activation(
                    out=sq2[:, 0:n_t], in_=xs1[:, fc * 512: fc * 512 + n_t],
                    func=AF.Square),
                    wait=[l2] if fc == 0 else [pe_st], inc="ac")
                def f_stm(e, fc=fc, n_t=n_t):
                    ins_ = e.matmul(out=ps_G2[0:1, 0:n_t], lhsT=ones1[:],
                             rhs=xs1[:, fc * 512: fc * 512 + n_t],
                             start=fc == 0, stop=fc == 7)
                    ins_ = e.matmul(out=ps_G2[0:1, 512:512 + n_t], lhsT=ones1[:],
                             rhs=sq2[:, 0:n_t], start=fc == 0, stop=fc == 7)
                    return ins_
                pe_st = S.emit("tensor", f_stm,
                               wait=[sq_prev, prev_st if fc == 0 else None], inc="pe")
            v1 = S.emit("vector", lambda e, n_t=n_t: e.tensor_scalar_mul(
                out=strow[0:1, 0:n_t], in0=ps_G2[0:1, 0:n_t], scalar1=1.0 / 1024),
                wait=[pe_st], inc="ve")
            v2 = S.emit("vector", lambda e, n_t=n_t: e.tensor_scalar_mul(
                out=strow[0:1, 512:512 + n_t], in0=ps_G2[0:1, 512:512 + n_t],
                scalar1=1.0 / 1024), wait=[v1], inc="ve")
            v3 = S.emit("vector", lambda e, n_t=n_t: e.tensor_mul(
                out=acbc[0:1, 0:n_t], in0=strow[0:1, 0:n_t], in1=strow[0:1, 0:n_t]),
                wait=[v2], inc="ve")
            v4 = S.emit("vector", lambda e, n_t=n_t: e.tensor_sub(
                out=strow[0:1, 512:512 + n_t], in0=strow[0:1, 512:512 + n_t],
                in1=acbc[0:1, 0:n_t]), wait=[v3], inc="ve")
            v5a = S.emit("scalar", lambda e, n_t=n_t: e.activation(
                out=strow[0:1, 512:512 + n_t], in_=strow[0:1, 512:512 + n_t],
                func=AF.Ln, bias=cst[0:1, 0:1]), wait=[v4], inc="ac")
            v5 = S.emit("scalar", lambda e, n_t=n_t: e.activation(
                out=strow[0:1, 512:512 + n_t], in_=strow[0:1, 512:512 + n_t],
                func=AF.Exp, scale=cst[0:1, 3:4]), wait=[v5a], inc="ac")
            v6 = S.emit("vector", lambda e, n_t=n_t: e.tensor_mul(
                out=strow[0:1, 0:n_t], in0=strow[0:1, 0:n_t],
                in1=strow[0:1, 512:512 + n_t]), wait=[v5], inc="ve")
            v7 = S.emit("vector", lambda e, n_t=n_t: e.tensor_scalar_mul(
                out=strow[0:1, 0:n_t], in0=strow[0:1, 0:n_t], scalar1=-1.0),
                wait=[v6], inc="ve")
            d1 = S.dma("scalar", AP(ac_sc, 0, [[1024, 1], [512, 2], [1, n_t]]),
                       AP(strow, 0, [[1024, 1], [512, 2], [1, n_t]]),
                       wait=[v7], ch="acs")
            d2 = S.dma("sync", AP(acbc, 0, [[1024, 128], [512, 2], [1, n_t]]),
                       AP(ac_sc, 0, [[0, 128], [512, 2], [1, n_t]]),
                       wait=[d1], ch="acr")
            hdone = []
            for fc in range(8):
                h1 = S.emit("vector", lambda e, fc=fc, n_t=n_t: e.tensor_mul(
                    out=h_sb[:, fc * 512: fc * 512 + n_t],
                    in0=xs1[:, fc * 512: fc * 512 + n_t],
                    in1=acbc[:, 512:512 + n_t]),
                    wait=[d2] if fc == 0 else [], inc="ve")
                hdone.append(S.emit("vector", lambda e, fc=fc, n_t=n_t: e.tensor_add(
                    out=h_sb[:, fc * 512: fc * 512 + n_t],
                    in0=h_sb[:, fc * 512: fc * 512 + n_t],
                    in1=acbc[:, 0:n_t]), wait=[h1], inc="ve"))
            glu_done = []
            for jj in range(NKC2):
                psA = ps_a if jj % 2 == 0 else ps_c
                psG = ps_b if jj % 2 == 0 else ps_d
                def f_w1(e, jj=jj, psA=psA, psG=psG, n_t=n_t):
                    for kc in range(8):
                        rhs = h_sb[:, kc * 512: kc * 512 + n_t]
                        ins_ = e.matmul(out=psA[:, 0:n_t],
                                 lhsT=w1s[kc][:, (2 * jj) * 128:(2 * jj + 1) * 128],
                                 rhs=rhs, start=kc == 0, stop=kc == 7)
                        ins_ = e.matmul(out=psG[:, 0:n_t],
                                 lhsT=w1s[kc][:, (2 * jj + 1) * 128:(2 * jj + 2) * 128],
                                 rhs=rhs, start=kc == 0, stop=kc == 7)
                    return ins_
                m = S.emit("tensor", f_w1, wait=[hdone[-1],
                           glu_done[-2] if len(glu_done) >= 2 else None], inc="pe")
                ga = gact2[jj % 2]
                g1 = S.emit("scalar", lambda e, jj=jj, psG=psG, n_t=n_t, ga=ga: e.activation(
                    out=ga[:, 0:n_t], in_=psG[:, 0:n_t],
                    func=AF.Gelu, bias=b1es[:, 2 * jj + 1: 2 * jj + 2]), wait=[m], inc="ac")
                glu_done.append(S.emit("vector", lambda e, jj=jj, psA=psA, n_t=n_t, ga=ga:
                    e.scalar_tensor_tensor(out=glu_sb[jj][:, 0:n_t],
                        in0=psA[:, 0:n_t], scalar=b1es[:, 2 * jj: 2 * jj + 1],
                        in1=ga[:, 0:n_t], op0=ALU.add, op1=ALU.mult),
                    wait=[g1], inc="ve"))
            x2done = []
            asmdone = []
            for fc in range(8):
                ping = ps_G2[:, (fc % 2) * 512:(fc % 2) * 512 + 512]
                lr = S.dma("sync", res2[fc % 2][:, 0:n_t],
                           AP(in_f32, fc * 128 * in_n + cin, [[in_n, 128], [1, n_t]]),
                           wait=[x2done[-2] if len(x2done) >= 2 else glu_done[1]],
                           ch=f"lr{fc % 2}")
                def f_w2b(e, fc=fc, ping=ping, n_t=n_t):
                    for kc in range(NKC2):
                        ins_ = e.matmul(out=ping[:, 0:n_t],
                                 lhsT=w2s[kc][:, fc * 128: (fc + 1) * 128],
                                 rhs=glu_sb[kc][:, 0:n_t],
                                 start=kc == 0, stop=kc == NKC2 - 1)
                    return ins_
                m = S.emit("tensor", f_w2b, wait=[glu_done[-1],
                           x2done[-2] if len(x2done) >= 2 else None], inc="pe")
                x2done.append(S.emit("vector", lambda e, fc=fc, ping=ping, n_t=n_t:
                    e.scalar_tensor_tensor(
                        out=AP(x2st, fc * 512, [[4096, 128], [1, n_t]]),
                        in0=ping[:, 0:n_t], scalar=b2c[:, fc:fc + 1],
                        in1=res2[fc % 2][:, 0:n_t], op0=ALU.add, op1=ALU.add),
                    wait=[m, lr], inc="ve"))
                if assemble:
                    la = S.dma("sync", att_t[fc % 2][:, 0:n_t],
                               AP(ao_sc, fc * 128 * OWN + g0 * 128, [[OWN, 128], [1, n_t]]),
                               wait=[asmdone[-2] if len(asmdone) >= 2 else prev_la],
                               ch=f"la{fc % 2}")
                    asmdone.append(S.emit("vector", lambda e, fc=fc, n_t=n_t:
                        e.tensor_add(
                            out=AP(x2st, fc * 512, [[4096, 128], [1, n_t]]),
                            in0=AP(x2st, fc * 512, [[4096, 128], [1, n_t]]),
                            in1=att_t[fc % 2][:, 0:n_t]),
                        wait=[x2done[-1], la], inc="ve"))
            if not assemble:
                def f_cb(e, n_t=n_t):
                    return e.tensor_copy(
                        out=AP(h_sb, 0, [[4096, 128], [512, 8], [1, n_t]]),
                        in_=AP(x2st, 0, [[4096, 128], [512, 8], [1, n_t]]))
                cb = S.emit("gpsimd", f_cb, wait=x2done, inc="gp")
                s1 = S.dma("sync", fm_write(out_f32, out_n, g0 * 128, n_t),
                           AP(x2st, 0, [[4096, 128], [512, 8], [1, n_t]]), wait=x2done,
                           ch="fs1")
                s2 = S.dma("sync", fm_write(out_b16, out_n, g0 * 128, n_t),
                           AP(h_sb, 0, [[4096, 128], [512, 8], [1, n_t]]), wait=[cb],
                           ch="fs2")
                prev_grp = s2
                prev_st = s2
                prev_s1 = s1
            else:
                pa = None
                for tt in range(gw):
                    lx = S.dma("sync", xab[tt % 2][:],
                               AP(x_all, (4096 + g0 * 128 + tt * 128) * D, [[D, 128], [1, D]]),
                               wait=[pa if pa else asmdone[-1]], ch=f"lx{tt % 2}")
                    def f_at(e, tt=tt):
                        for fc in range(8):
                            ins_ = e.transpose(out=ps_E2[:, fc * 128: (fc + 1) * 128],
                                        in_=AP(x2st, fc * 512 + tt * 128, [[4096, 128], [1, 128]]),
                                        identity=idf[:])
                        return ins_
                    m = S.emit("tensor", f_at, wait=[asmdone[-1], pa], inc="pe")
                    av = S.emit("vector", lambda e, tt=tt: e.tensor_add(
                        out=xab[tt % 2][:], in0=ps_E2[:, 0:1024], in1=xab[tt % 2][:]),
                        wait=[m, lx], inc="ve")
                    pa = S.dma("sync", AP(out_t, (g0 + tt) * 128 * D, [[D, 128], [1, D]]),
                               xab[tt % 2][:], wait=[av], ch="out")
                prev_la = pa
                prev_grp = pa
                prev_st = pa
            g0 += gw
        S.barrier()

    if upto >= 3:
        stage_fa(0, xn_sc, XN_N, (FA0_Q0 - 512) - (-4096), xn_sc, XN_N, False,
                 FA0_Q0 - (-4096), x1f_sc, x1b_sc, X1_N, FA0_NQ,
                 (FA0_Q0 - 512) - TAB0, 0)
    if upto >= 4:
        stage_fb(0, x1f_sc, x1b_sc, X1_N, FB0_Q0 - FA0_Q0, FB0_NQ,
                 x2f_sc, x2b_sc, X2_N)
    if upto >= 5:
        stage_fa(1, x2b_sc, X2_N, (FA1_Q0 - 512) - FB0_Q0, x2f_sc, X2_N, True,
                 FA1_Q0 - FB0_Q0, x3f_sc, x3b_sc, X3_N, FA1_NQ,
                 (FA1_Q0 - 512) - TAB0, FA0_NQ)
    if upto >= 6:
        stage_fb(1, x3f_sc, x3b_sc, X3_N, FB1_Q0 - FA1_Q0, FB1_NQ,
                 None, None, None, assemble=True)

    S.q["sync"].append((tuple((n, S.c[n]) for n in sorted(S.dma_chans)
                              if S.c[n] > 0), None, None, False))
    S.realize()
    return nc



def _host_tables():
    hi = np.arange(0, 64, 2, dtype=np.float64)
    inv_freq = 1.0 / 10000.0 ** (hi / 64)
    t = np.arange(T, dtype=np.float64)
    fr = t[:, None] * inv_freq[None, :]
    freqs = np.concatenate([fr, fr], axis=-1)
    base = (hi + 0.4 * 64) / (1.4 * 64)
    power = (t - T // 2) / float(WIN // 2)
    scale = base[None, :] ** power[:, None]
    scale = np.concatenate([scale, scale], axis=-1)
    cos, sin = np.cos(freqs), np.sin(freqs)
    sgn = np.where(np.arange(64) < 32, -1.0, 1.0)[None, :]
    csq = (8.0 * cos * scale).astype(np.float32)
    ssq = (8.0 * sgn * sin * scale).astype(np.float32)
    csk = (cos / scale).astype(np.float32)
    ssk = (sgn * sin / scale).astype(np.float32)
    return csq, ssq, csk, ssk


def _masks_for(t0):
    out = np.zeros((NMASK, 128, 640), dtype=np.float32)
    p = np.arange(128)[:, None]
    j = np.arange(640)[None, :]
    idx = 0
    for (q0, nqq) in ((FA0_Q0, FA0_NQ), (FA1_Q0, FA1_NQ)):
        for s in range(nqq):
            i0 = t0 + q0 + s * 128
            lower = np.maximum(p, 512 - i0)
            empty = lower > p + 512
            lo = np.where(empty, p, lower)
            valid = (j >= lo) & (j <= p + 512)
            out[idx] = np.where(valid, 1.0, 0.0)
            idx += 1
    return out.astype(ml_dtypes.bfloat16)


def _perm_pad_w1(w1, b1):
    w1n = np.zeros((D, W1C), dtype=w1.dtype)
    b1n = np.zeros((W1C,), dtype=b1.dtype)
    for jj in range(NKC2):
        a0, a1 = jj * 128, min((jj + 1) * 128, HID)
        g0_, g1_ = HID + jj * 128, min(HID + (jj + 1) * 128, 2 * HID)
        w1n[:, 2 * jj * 128: 2 * jj * 128 + (a1 - a0)] = w1[:, a0:a1]
        w1n[:, (2 * jj + 1) * 128: (2 * jj + 1) * 128 + (g1_ - g0_)] = w1[:, g0_:g1_]
        b1n[2 * jj * 128: 2 * jj * 128 + (a1 - a0)] = b1[a0:a1]
        b1n[(2 * jj + 1) * 128: (2 * jj + 1) * 128 + (g1_ - g0_)] = b1[g0_:g1_]
    return w1n, b1n


_NC_CACHE = {}


def kernel(L0_attn_x, L0_attn_w_attn, L0_attn_w_proj, L0_ffn_wqkv,
           L0_ffn_q_scale, L0_ffn_k_scale, L0_ffn_wo, L0_ffn_ln_g, L0_ffn_ln_b,
           L0_ffn_w1, L0_ffn_b1, L0_ffn_w2, L0_ffn_b2, _dump=(), _cores=None,
           _upto=6):
    x = np.ascontiguousarray(np.asarray(L0_attn_x, dtype=np.float32))
    csq, ssq, csk, ssk = _host_tables()
    w1pl, b1pl, w2pl = [], [], []
    for l in range(2):
        w1n, b1n = _perm_pad_w1(np.asarray(L0_ffn_w1)[l], np.asarray(L0_ffn_b1)[l])
        w1pl.append(w1n)
        b1pl.append(b1n.reshape(44, 128).T)
        w2n = np.zeros((HIDP, D), dtype=np.float32)
        w2n[:HID] = np.asarray(L0_ffn_w2)[l]
        w2pl.append(w2n)
    w1p_a = np.ascontiguousarray(np.stack(w1pl).astype(np.float32))
    b1p_a = np.ascontiguousarray(np.stack(b1pl).astype(np.float32))
    w2p_a = np.ascontiguousarray(np.stack(w2pl))
    lbv_a = np.ascontiguousarray(np.stack(
        [np.asarray(L0_ffn_ln_b)[l].reshape(8, 128).T for l in range(2)]
    ).astype(ml_dtypes.bfloat16))
    lgv_a = np.ascontiguousarray(np.stack(
        [np.asarray(L0_ffn_ln_g)[l].reshape(8, 128).T for l in range(2)]).astype(np.float32))
    b2v_a = np.ascontiguousarray(np.stack(
        [np.asarray(L0_ffn_b2)[l].reshape(8, 128).T for l in range(2)]).astype(np.float32))
    ident_bm = np.eye(128, dtype=ml_dtypes.bfloat16)
    ident_fm = np.eye(128, dtype=np.float32)
    trilv = np.ascontiguousarray(
        np.tril(np.ones((128, 128), dtype=np.float32)).T.astype(ml_dtypes.bfloat16))
    one16_a = np.ones((1, 16), dtype=ml_dtypes.bfloat16)
    crow_a = np.array([[1e-5, 1e-6, 1e-24, -0.5, -1.0, 1.0 / 1024, 0.0, 0.0]],
                      dtype=np.float32)

    in_maps = []
    cores = list(range(8)) if _cores is None else list(_cores)
    for core in cores:
        bi, h = core // 2, core % 2
        t0 = OWN * h
        xa = np.zeros((XN_N, D), dtype=np.float32)
        lo = t0 - 4096
        src_lo = max(lo, 0)
        xa[src_lo - lo: XN_N] = x[bi, src_lo: t0 + 4096]
        tabrows = np.arange(t0 + TAB0, t0 + TAB0 + TAB_N)
        tsel = np.clip(tabrows, 0, T - 1)
        zero = (tabrows < 0) | (tabrows >= T)
        def tab(a):
            v = a[tsel].copy()
            v[zero] = 0.0
            return np.ascontiguousarray(v)
        m = {
            "x_all": xa,
            "w_attn": np.ascontiguousarray(np.asarray(L0_attn_w_attn, dtype=np.float32)),
            "w_proj": np.ascontiguousarray(np.asarray(L0_attn_w_proj, dtype=np.float32)),
            "wqkv": np.ascontiguousarray(np.asarray(L0_ffn_wqkv, dtype=np.float32)),
            "wo": np.ascontiguousarray(np.asarray(L0_ffn_wo, dtype=np.float32)),
            "w1p": w1p_a, "w2p": w2p_a, "b1p": b1p_a,
            "lbv": lbv_a, "lgv": lgv_a, "b2v": b2v_a,
            "qsr": np.ascontiguousarray(np.asarray(L0_ffn_q_scale, dtype=np.float32).reshape(2, 1, 64)),
            "ksr": np.ascontiguousarray(np.asarray(L0_ffn_k_scale, dtype=np.float32).reshape(2, 1, 64)),
            "csq": tab(csq), "ssq": tab(ssq), "csk": tab(csk), "ssk": tab(ssk),
            "mks": _masks_for(t0),
            "ident_b": ident_bm, "ident_f": ident_fm, "trilT": trilv,
            "smask": np.full((1, 16), 1.0 if h == 1 else 0.0, dtype=ml_dtypes.bfloat16),
            "one16": one16_a, "crow": crow_a,
        }
        in_maps.append(m)

    key = (tuple(sorted(_dump)), _upto)
    if key not in _NC_CACHE:
        _NC_CACHE[key] = build_nc(dump=_dump, upto=_upto)
    nc = _NC_CACHE[key]
    res = run_bass_kernel_spmd(nc, in_maps, core_ids=list(range(len(cores))))
    if _dump:
        return res
    out = np.empty((B, T, D), dtype=np.float32)
    for i, core in enumerate(cores):
        bi, h = core // 2, core % 2
        out[bi, OWN * h: OWN * (h + 1)] = res.results[i]["out"]
    return out
